# version 07
# speedup vs baseline: 2.2192x; 1.0074x over previous
import numpy as np
import ml_dtypes
from contextlib import ExitStack

import concourse.bass as bass
import concourse.mybir as mybir
from concourse.bass_types import AP
from concourse.bass_utils import run_bass_kernel_spmd

F32 = mybir.dt.float32
BF16 = mybir.dt.bfloat16
I32 = mybir.dt.int32
AO = mybir.AluOpType
AF = mybir.ActivationFunctionType
AX = mybir.AxisListType.X

D = 1024
T_CORE = 2048
T_FULL = 8192
NCH = 16
NTI = 4
EPS = 1e-5
RG = [[0, 1, 2, 3], [4, 5, 6, 7]]


def batcher_steps(n=32):
    steps = []
    p = 1
    while p < n:
        k = p
        while k >= 1:
            los = []
            j = k % p
            while j <= n - 1 - k:
                for i in range(0, min(k, n - j - k)):
                    lo = i + j
                    if (lo // (2 * p)) == ((lo + k) // (2 * p)):
                        los.append(lo)
                j += 2 * k
            steps.append((k, sorted(los)))
            k //= 2
        p *= 2
    return steps


def lo_pattern(los):
    los = list(los)
    r = 1
    while r < len(los) and los[r] == los[0] + r:
        r += 1
    runs = [los[i * r] for i in range(len(los) // r)]
    for i, st in enumerate(runs):
        assert los[i * r:(i + 1) * r] == list(range(st, st + r))
    if len(runs) == 1:
        return los[0], [(1, r)]
    s1 = runs[1] - runs[0]
    n1 = 1
    while n1 < len(runs) and runs[n1] == runs[0] + n1 * s1:
        n1 += 1
    if n1 == len(runs):
        return runs[0], [(s1, n1), (1, r)]
    s2 = runs[n1] - runs[0]
    blocks = [runs[i * n1] for i in range(len(runs) // n1)]
    for i, b in enumerate(blocks):
        assert runs[i * n1:(i + 1) * n1] == [b + jj * s1 for jj in range(n1)]
    return runs[0], [(s2, len(blocks)), (s1, n1), (1, r)]


SORT_STEPS = [(k, lo_pattern(los)) for k, los in batcher_steps(32)]


def carve(base, off, dims):
    pstep, pcount = base.ap[0]
    ap = [[pstep, pcount], [32, 16]] + [[s_, n_] for (s_, n_) in dims]
    return AP(base.tensor, base.offset + off, ap)



class _EngW:
    _SKIP = {"wait_ge", "nop", "engine_nop"}

    def __init__(self, eng):
        object.__setattr__(self, "_e", eng)
        object.__setattr__(self, "last", None)

    def __getattr__(self, n):
        a = getattr(self._e, n)
        if callable(a) and not n.startswith("_"):
            def call(*args, **kw):
                r = a(*args, **kw)
                if n not in _EngW._SKIP and r is not None and hasattr(r, "then_inc"):
                    object.__setattr__(self, "last", r)
                return r
            return call
        return a


class Ctr:
    __slots__ = ("sem", "n")

    def __init__(self, sem):
        self.sem = sem
        self.n = 0

    def inc(self, instr, k=1):
        instr.then_inc(self.sem, k)
        self.n += k
        return instr

    def dma(self, instr):
        instr.then_inc(self.sem, 16)
        self.n += 16
        return instr

    def wait(self, eng, n=None):
        eng.wait_ge(self.sem, self.n if n is None else n)


def build_program(debug_phase=None):
    nc = bass.Bass("TRN2", target_bir_lowering=False, debug=False)

    def din(name, shape, dt=F32):
        return nc.dram_tensor(name, shape, dt, kind="ExternalInput").ap()

    x0_d = din("x0", [T_CORE, D])
    kp_d = din("kp", [32, D])
    wqo_d = din("wqo", [D, D]); wko_d = din("wko", [D, D])
    wvo_d = din("wvo", [D, D]); woo_d = din("woo", [D, D])
    bq_d = din("bq", [8, 128]); bk_d = din("bk", [8, 128])
    bv_d = din("bv", [1, D]); bo_d = din("bo", [1, D])
    wqi_d = din("wqi", [D, 256]); wki_d = din("wki", [D, 256])
    wvi_d = din("wvi", [D, 256]); woi_d = din("woi", [D, D])
    rw_d = din("rw", [D, 8])
    fc1_d = din("fc1", [8 * D, 2048])
    fc2_d = din("fc2", [8 * 2048, D])
    mask_d = din("masks", [128, 4 * 512])
    kv_d = din("kvec", [128, 512])
    idb_d = din("ident_bf", [128, 128], BF16)
    idf_d = din("ident_f32", [128, 128])
    toff_d = din("toffs", [1, 16], I32)
    soff_d = din("soffs", [1, 16], I32)
    out_d = nc.dram_tensor("out", [T_CORE, D], F32, kind="ExternalOutput").ap()

    ago_in = nc.dram_tensor("ago_in", [T_CORE, D], BF16).ap()
    ago_out = nc.dram_tensor("ago_out", [T_FULL, D], BF16).ap()
    ag1_in = nc.dram_tensor("ag1_in", [4 * D, 512], BF16).ap()
    ag1_out = nc.dram_tensor("ag1_out", [16 * D, 512], BF16).ap()
    ag2_in = nc.dram_tensor("ag2_in", [1024, 2048], BF16).ap()
    ag2_out = nc.dram_tensor("ag2_out", [4096, 2048], BF16).ap()
    x1_d = nc.dram_tensor("x1_spill", [T_CORE, D], F32).ap()

    dbg_d = None
    if debug_phase is not None:
        dbg_d = nc.dram_tensor("dbg", [128, NCH * D], F32, kind="ExternalOutput").ap()

    ctx = ExitStack()

    def sbt(name, shape, dt):
        return ctx.enter_context(nc.sbuf_tensor(name, shape, dt))

    def pst(name, shape, dt=F32):
        return ctx.enter_context(nc.psum_tensor(name, shape, dt))

    bufA = sbt("bufA", [128, NCH * D], F32)

    def A(c):
        return bufA[:, c * D:(c + 1) * D]

    wpool = sbt("wpool", [128, 4 * 8 * D], BF16)
    WQO = wpool[:, 0:8 * D].rearrange("p (c n) -> p c n", c=8)
    WKO = wpool[:, 8 * D:16 * D].rearrange("p (c n) -> p c n", c=8)
    WVO = wpool[:, 16 * D:24 * D].rearrange("p (c n) -> p c n", c=8)
    WOO = wpool[:, 24 * D:32 * D].rearrange("p (c n) -> p c n", c=8)
    FC1H = [wpool[:, (2 * i) * 8 * D:(2 * i + 1) * 8 * D].rearrange("p (c n) -> p c n", c=8)
            for i in range(2)]
    FC2H = [wpool[:, (2 * i + 1) * 8 * D:(2 * i + 2) * 8 * D].rearrange("p (c n) -> p c n", c=8)
            for i in range(2)]

    QTI = wpool[:, 0:2 * T_FULL].rearrange("p (c n) -> p c n", c=2)
    KTI = wpool[:, 2 * T_FULL:4 * T_FULL].rearrange("p (c n) -> p c n", c=2)

    vpool = sbt("vpool", [128, 64 * 260], BF16)
    V_I = vpool[:].rearrange("p (c n) -> p c n", c=64)
    RT = vpool[:, :8 * T_CORE].rearrange("p (c n) -> p c n", c=8)
    ATP = vpool[:, :4096].rearrange("p (c n) -> p c n", c=8)

    wqi_s = sbt("wqi_s", [128, 8 * 256], BF16).rearrange("p (c n) -> p c n", c=8)
    wki_s = sbt("wki_s", [128, 8 * 256], BF16).rearrange("p (c n) -> p c n", c=8)
    rw_s = sbt("rw_s", [128, 8 * 8], BF16).rearrange("p (c n) -> p c n", c=8)
    woi_s = wpool[:, 2 * T_FULL:2 * T_FULL + 8 * D].rearrange("p (c n) -> p c n", c=8)

    kpT_s = vpool[:, 0:256]
    KPT = kpT_s[:].rearrange("p (c n) -> p c n", c=8)
    kT_o = vpool[:, 16384:16640]
    v_o = sbt("v_o", [64, D], BF16)
    bqc = sbt("bqc", [128, 8], F32)
    bkc = sbt("bkc", [128, 8], F32)
    bv_r = sbt("bv_r", [1, D], BF16)
    bo_r = sbt("bo_r", [1, D], BF16)

    ident = sbt("ident", [128, 128], BF16)
    identf = vpool[:, 256:512].bitcast(F32)
    ones_bf = sbt("ones_bf", [1, 128], BF16)
    toff_s = sbt("toff_s", [1, 16], I32)
    soff_s = sbt("soff_s", [1, 16], I32)

    qT_o = vpool[:, 4096:8192].rearrange("p (c n) -> p c n", c=8)
    z_both = sbt("z_both", [128, 1024], F32)
    z_sp = [z_both[:, 512 * i:512 * (i + 1)] for i in range(2)]
    masks = z_both[:].bitcast(BF16)
    tau = sbt("tau", [128, 16], F32)
    ksup = sbt("ksup", [128, 16], F32)
    csel = sbt("csel", [128, 16], F32)
    a_bf0 = sbt("a_bf0", [128, 512], BF16)
    a_bf = [a_bf0, a_bf0]
    at_lo = vpool[0:64, 12288:14336].rearrange("p (c n) -> p c n", c=4)
    at_hi = vpool[0:64, 14336:16384].rearrange("p (c n) -> p c n", c=4)
    aoT_s = vpool[:, 8192:12288].rearrange("p (c n) -> p c n", c=8)
    hT_t = [sbt(f"hT_{i}", [128, 8 * 512], BF16) for i in range(2)]
    HTT = [h[:].rearrange("p (c n) -> p c n", c=8) for h in hT_t]
    a_nat = [hT_t[0][:, 3072:4096], hT_t[1][:, 2048:3072]]

    sumv = sbt("sumv", [128, 16], F32)
    sumsq = sbt("sumsq", [128, 16], F32)
    mean_v = sbt("mean_v", [128, 16], F32)
    var_v = sbt("var_v", [128, 16], F32)
    r1_v = sbt("r1_v", [128, 16], F32)
    sd_v = sbt("sd_v", [128, 16], F32)
    m1_v = sbt("m1_v", [128, 16], F32)
    c2_v = sbt("c2_v", [128, 16], F32)
    scr1k = sbt("scr1k", [128, D], F32)
    scr1k2 = sbt("scr1k2", [128, D], F32)
    scr16 = sbt("scr16", [128, 16], F32)
    wvi_s = scr1k[:].bitcast(BF16).rearrange("p (c n) -> p c n", c=8)
    kp_s = scr1k2[0:32, 0:512].bitcast(BF16)
    bq_st = scr1k[0:8, 0:128]
    bk_st = scr1k[0:8, 128:256]
    den_s = scr1k2[0:1, 0:512]
    rec_s = scr1k2[0:1, 512:1024]
    recb = scr1k2[64:128, 0:512]
    ob_s = scr1k[:, 0:256].bitcast(BF16)
    lg8 = sbt("lg8", [128, 8], F32)
    pr8 = sbt("pr8", [128, 8], F32)
    pm8 = sbt("pm8", [128, 8], F32)
    msk8 = sbt("msk8", [128, 8], F32)
    msk8b = sbt("msk8b", [128, 8], F32)
    tv16 = sbt("tv16", [128, 16], F32)

    srt = hT_t[0][:].bitcast(F32)[:, 0:512]
    kvec = hT_t[0][:].bitcast(F32)[:, 1024:1536]
    ones_f = kvec[0:1, 0:64]
    srtmp = hT_t[0][:].bitcast(F32)[:, 512:1024]
    cum = hT_t[1][:].bitcast(F32)[:, 0:512]
    sup = hT_t[1][:].bitcast(F32)[:, 512:1024]
    e_bf = [hT_t[0][:, 512 * i:512 * (i + 1)] for i in range(4)]

    g_e = sbt("g_e", [128, 16 * 8], F32).rearrange("p (c n) -> p c n", c=16)
    h1sq = scr1k2[:, 512:768].bitcast(BF16)
    r_bf = scr1k2[:, 0:512].bitcast(BF16)
    x1_t = [hT_t[i][:].bitcast(F32)[:, 0:1024] for i in range(2)]

    ps_tr = pst("ps_tr", [128, 1024], BF16)
    ps_g1 = pst("ps_g1", [128, 512])
    ps_g2 = pst("ps_g2", [128, 512])
    ps_sA = [pst(f"ps_sA{i}", [128, 512]) for i in range(2)]
    ps_sB = [pst(f"ps_sB{i}", [128, 512]) for i in range(2)]
    ps_oA = pst("ps_oA", [128, 512])
    ps_oB = ps_g2
    PS_G = [ps_g1, ps_g2]

    names = ["ld", "wld", "sv", "ss", "st2", "cc", "sy", "pv", "pw", "fcl", "fc2"]
    C = {n: Ctr(ctx.enter_context(nc.semaphore(n))) for n in names}
    ld, wld, sv, ss, st2 = C["ld"], C["wld"], C["sv"], C["ss"], C["st2"]
    cc, sy, pv, fcl, pw, fc2 = C["cc"], C["sy"], C["pv"], C["fcl"], C["pw"], C["fc2"]

    M = {}

    stop = {"v": False}

    with ctx:
        with nc.Block() as block:
            def _wrap(reg):
                return lambda f: reg(lambda e: f(_EngW(e)))
            T_, V_, S_, G_, Y_ = (_wrap(block.tensor), _wrap(block.vector),
                                  _wrap(block.scalar), _wrap(block.gpsimd),
                                  _wrap(block.sync))

            def dump(tag, srcs):
                if debug_phase != tag:
                    return False

                def _y(y):
                    sv.wait(y)
                    ss.wait(y)
                    st2.wait(y)
                    off = 0
                    for src in srcs:
                        n = src.shape[-1]
                        sy.dma(y.dma_start(out=dbg_d[:src.shape[0], off:off + n], in_=src))
                        off += n
                    sy.wait(y)
                G_(_y)
                stop["v"] = True
                return True

            def y0(y):
                ld.dma(y.dma_start(out=bq_st, in_=bq_d[:, :]))
                ld.dma(y.dma_start(out=bk_st, in_=bk_d[:, :]))
                ld.dma(y.dma_start(out=kvec, in_=kv_d[:, :]))
                ld.dma(y.dma_start(out=ident[:], in_=idb_d[:, :]))
                ld.dma(y.dma_start(out=identf, in_=idf_d[:, :]))
                ld.dma(y.dma_start(out=toff_s[:], in_=toff_d[:, :]))
                ld.dma(y.dma_start(out=soff_s[:], in_=soff_d[:, :]))
            Y_(y0)
            M["p0_ldc"] = ld.n

            def y0x(y):
                for ti in range(NTI):
                    ld.dma(y.dma_start(
                        out=bufA[:, 4096 * ti:4096 * (ti + 1)].rearrange(
                            "p (c n) -> p c n", c=4),
                        in_=x0_d.rearrange("(t c p) d -> p t c d", p=128, t=NTI)[:, ti]))
                    M[f"p0_x0_{ti}"] = ld.n
            Y_(y0x)
            M["p0_ld"] = ld.n

            REGS = {}

            def y_regs(y):
                ld.wait(y, M["p0_ldc"])
            Y_(y_regs)
            toff_reg = ctx.enter_context(nc.scalar.register("toffr"))
            soff_reg = ctx.enter_context(nc.sync.register("soffr"))

            g0_wld0 = wld.n

            def g0(g):
                wld.dma(g.dma_start(out=kp_s, in_=kp_d[:, :]))
                wld.dma(g.dma_start(out=bv_r[:], in_=bv_d[:, :]))
                wld.dma(g.dma_start(out=bo_r[:], in_=bo_d[:, :]))
                for w_s, w_d in [(WKO, wko_d), (WVO, wvo_d), (WQO, wqo_d),
                                 (WOO, woo_d), (wqi_s, wqi_d),
                                 (wki_s, wki_d), (rw_s, rw_d)]:
                    wld.dma(g.dma_start(out=w_s, in_=w_d.rearrange("(c p) n -> p c n", p=128)))
            G_(g0)
            M["p0_kp"] = g0_wld0 + 3 * 16
            M["p0_wko"] = g0_wld0 + 4 * 16
            M["p0_wvo"] = g0_wld0 + 5 * 16
            M["p0_wqo"] = g0_wld0 + 6 * 16
            M["p0_woo"] = g0_wld0 + 7 * 16
            M["p0_wld"] = wld.n

            def v0(v):
                sv.inc(v.memset(ones_bf[:], 1.0))
            V_(v0)
            M["p0_sv"] = sv.n

            if dump("p00", [kvec, identf]):
                return nc

            def t0(t):
                ld.wait(t, M["p0_ldc"])
                wld.wait(t, M["p0_kp"])
                t.transpose(ps_g1[:, 0:8], in_=bq_st, identity=identf[0:8, 0:8])
                t.transpose(ps_g2[:, 0:8], in_=bk_st, identity=identf[0:8, 0:8])
                for j in range(8):
                    t.transpose(ps_tr[:, 32 * j:32 * j + 32],
                                in_=kp_s[:, 128 * j:128 * (j + 1)],
                                identity=ident[0:32, 0:32])
                st2.inc(t.last)
            T_(t0)
            M["p0_tr"] = st2.n

            if dump("p01", [kvec]):
                return nc

            def s0(s):
                st2.wait(s, M["p0_tr"])
                s.copy(out=bqc[:, :], in_=ps_g1[:, 0:8])
                s.copy(out=bkc[:, :], in_=ps_g2[:, 0:8])
                ss.inc(s.copy(out=kpT_s[:, :], in_=ps_tr[:, 0:256]))
            S_(s0)
            M["p0_ss"] = ss.n

            if dump("p02", [bqc[:], bkc[:]]):
                return nc

            def t1(t):
                ss.wait(t, M["p0_ss"])
                wld.wait(t, M["p0_wko"])
                for j in range(8):
                    if j >= 2:
                        sv.wait(t, M["p0_sv"] + (j - 1))
                    for dc in range(8):
                        t.matmul(ps_sA[j % 2][:, 0:32],
                                 lhsT=WKO[:, dc, 128 * j:128 * (j + 1)],
                                 rhs=KPT[:, dc, :].opt(), start=(dc == 0), stop=(dc == 7))
                    st2.inc(t.last)
            T_(t1)
            M["p0_kt"] = st2.n

            def v1(v):
                for j in range(8):
                    st2.wait(v, M["p0_tr"] + j + 1)
                    sv.inc(v.tensor_scalar(out=kT_o[:, 32 * j:32 * j + 32],
                                           in0=ps_sA[j % 2][:, 0:32],
                                           scalar1=bkc[:, j:j + 1], scalar2=None, op0=AO.add))
            V_(v1)
            M["p0_sv2"] = sv.n

            if dump("p03", [kT_o[:, :]]):
                return nc

            def t2(t):
                sv.wait(t, M["p0_sv2"])
                wld.wait(t, M["p0_wvo"])
                for nh in range(2):
                    for j in range(8):
                        t.matmul(ps_sB[nh][0:32, :], lhsT=kT_o[:, 32 * j:32 * j + 32],
                                 rhs=WVO[:, j, 512 * nh:512 * (nh + 1)],
                                 start=(j == 0), stop=False)
                    t.matmul(ps_sB[nh][0:32, :], lhsT=ones_bf[0:1, 0:32],
                             rhs=bv_r[0:1, 512 * nh:512 * (nh + 1)], start=False, stop=True)
                st2.inc(t.last)
            T_(t2)
            M["p0_vo"] = st2.n

            def s1(s):
                st2.wait(s, M["p0_vo"])
                for nh in range(2):
                    s.copy(out=v_o[0:32, 512 * nh:512 * (nh + 1)], in_=ps_sB[nh][0:32, :])
                    ss.inc(s.copy(out=v_o[32:64, 512 * nh:512 * (nh + 1)],
                                  in_=ps_sB[nh][0:32, :]))
            S_(s1)
            M["p0_end_ss"] = ss.n

            if dump("p0", [kT_o[:, :], v_o[:, :]]):
                return nc

            for ti in range(NTI):
                cs = [4 * ti + k for k in range(4)]

                def v_st(v, cs=cs, ti=ti):
                    ld.wait(v, M[f"p0_x0_{ti}"])
                    for c in cs:
                        v.reduce_sum(out=sumv[:, c:c + 1], in_=A(c), axis=AX)
                    sv.inc(v.last)
                V_(v_st)

                def s_st(s, cs=cs, ti=ti):
                    ld.wait(s, M[f"p0_x0_{ti}"])
                    if ti == 0:
                        st2.wait(s, M["p0_tr"])
                    for c in cs:
                        s.activation(out=scr1k[:], in_=A(c), func=AF.Square,
                                     accum_out=sumsq[:, c:c + 1])
                    ss.inc(s.last)
                S_(s_st)
                M[f"p1_stats_{ti}"] = (sv.n, ss.n)

                def v_mz(v, cs=cs, ti=ti):
                    ss.wait(v, M[f"p1_stats_{ti}"][1])
                    sv.wait(v, M[f"p1_stats_{ti}"][0])
                    sl = slice(cs[0], cs[0] + 4)
                    v.tensor_scalar(out=mean_v[:, sl], in0=sumv[:, sl], scalar1=1.0 / D,
                                    scalar2=None, op0=AO.mult)
                    pv.inc(v.tensor_scalar(out=var_v[:, sl], in0=sumsq[:, sl],
                                           scalar1=1.0 / D, scalar2=None, op0=AO.mult))
                    pv.wait(v)
                    pv.inc(v.tensor_tensor(out=scr16[:, sl], in0=mean_v[:, sl],
                                           in1=mean_v[:, sl], op=AO.mult))
                    pv.wait(v)
                    pv.inc(v.tensor_tensor(out=var_v[:, sl], in0=var_v[:, sl],
                                           in1=scr16[:, sl], op=AO.subtract))
                    pv.wait(v)
                    sv.inc(v.tensor_scalar(out=var_v[:, sl], in0=var_v[:, sl], scalar1=EPS,
                                           scalar2=None, op0=AO.add))
                V_(v_mz)
                M[f"p1_var_{ti}"] = sv.n

                def s_rs(s, cs=cs, ti=ti):
                    sv.wait(s, M[f"p1_var_{ti}"])
                    sl = slice(cs[0], cs[0] + 4)
                    pw.inc(s.activation(out=scr16[:, sl], in_=var_v[:, sl], func=AF.Ln))
                    pw.wait(s)
                    ss.inc(s.activation(out=r1_v[:, sl], in_=scr16[:, sl], func=AF.Exp,
                                        scale=-0.5))
                S_(s_rs)
                M[f"p1_rstd_{ti}"] = ss.n

                for ci, c in enumerate(cs):
                    def v_ap(v, c=c, ci=ci, ti=ti):
                        if ci == 0:
                            ss.wait(v, M[f"p1_rstd_{ti}"])
                        key = f"p1_aTtr_{ti}_{ci - 2}" if ci >= 2 else (
                            f"p1_aTtr_{ti - 1}_{ci + 2}" if ti > 0 else None)
                        if key is not None:
                            st2.wait(v, M[key])
                        sv.inc(v.tensor_scalar(out=a_nat[ci % 2], in0=A(c),
                                               scalar1=mean_v[:, c:c + 1],
                                               scalar2=r1_v[:, c:c + 1],
                                               op0=AO.subtract, op1=AO.mult))
                    V_(v_ap)
                    M[f"p1_anat_{ti}_{ci}"] = sv.n

                    def t_tr(t, ti=ti, ci=ci):
                        sv.wait(t, M[f"p1_anat_{ti}_{ci}"])
                        kprev = (f"p1_aTev_{ti}_{ci - 1}" if ci > 0 else
                                 (f"p1_aev_{ti - 1}_3" if ti > 0 else None))
                        if kprev is not None:
                            ss.wait(t, M[kprev])
                        for j in range(8):
                            t.transpose(ps_tr[:, 128 * j:128 * (j + 1)],
                                        in_=a_nat[ci % 2][:, 128 * j:128 * (j + 1)],
                                        identity=ident[:])
                        st2.inc(t.last)
                    T_(t_tr)
                    M[f"p1_aTtr_{ti}_{ci}"] = st2.n

                    def s_ev(s, ti=ti, ci=ci):
                        st2.wait(s, M[f"p1_aTtr_{ti}_{ci}"])
                        ss.inc(s.copy(out=ATP[:, :, 128 * ci:128 * (ci + 1)],
                                      in_=ps_tr[:].rearrange("p (c n) -> p c n", c=8)))
                    S_(s_ev)
                    M[f"p1_aTev_{ti}_{ci}"] = ss.n
                    if ti == 0 and ci == 0 and dump("d_ln1", [a_nat[0]]):
                        return nc

                qev_start = sv.n

                def t_q(t, ti=ti, qev_start=qev_start):
                    ss.wait(t, M[f"p1_aTev_{ti}_3"])
                    if ti == 0:
                        wld.wait(t, M["p0_wqo"])
                    for j in range(8):
                        if j >= 2:
                            sv.wait(t, qev_start + (j - 1))
                        elif ti > 0:
                            ss.wait(t, M[f"p1_oev_{ti - 1}_3"])
                        for dc in range(8):
                            t.matmul(PS_G[j % 2][:, :],
                                     lhsT=WQO[:, dc, 128 * j:128 * (j + 1)],
                                     rhs=ATP[:, dc, :].opt(), start=(dc == 0), stop=(dc == 7))
                        st2.inc(t.last)
                T_(t_q)
                M[f"p1_qmm_{ti}"] = st2.n
                M[f"p1_qev_{ti}"] = qev_start + 8

                def v_qe(v, ti=ti):
                    for j in range(8):
                        st2.wait(v, M[f"p1_qmm_{ti}"] - 8 + j + 1)
                        sv.inc(v.tensor_scalar(out=qT_o[:, j, :].opt(), in0=PS_G[j % 2][:, :],
                                               scalar1=bqc[:, j:j + 1], scalar2=None,
                                               op0=AO.add))
                V_(v_qe)
                assert sv.n == M[f"p1_qev_{ti}"]
                if ti == 0 and dump("d_q", [vpool[:, 4096:8192]]):
                    return nc

                for ci, c in enumerate(cs):
                    def t_sc(t, ti=ti, ci=ci):
                        if ci == 0:
                            sv.wait(t, M[f"p1_qev_{ti}"])
                        kprev = (f"p1_zev_{ti}_{ci - 2}" if ci >= 2 else
                                 (f"p1_zev_{ti - 1}_{ci + 2}" if ti > 0 else None))
                        if kprev is not None:
                            ss.wait(t, M[kprev])
                        phA, phB = ps_sA[ci % 2], ps_sB[ci % 2]
                        for pj in range(8):
                            t.matmul(phA[:, 32 * pj:32 * pj + 32],
                                     lhsT=qT_o[0:64, pj, 128 * ci:128 * (ci + 1)].opt(),
                                     rhs=kT_o[0:64, 32 * pj:32 * pj + 32],
                                     start=True, stop=True)
                            t.matmul(phB[:, 32 * pj:32 * pj + 32],
                                     lhsT=qT_o[64:128, pj, 128 * ci:128 * (ci + 1)].opt(),
                                     rhs=kT_o[64:128, 32 * pj:32 * pj + 32],
                                     start=True, stop=True, tile_position=(64, 0))
                        st2.inc(t.last)
                    T_(t_sc)
                    M[f"p1_smm_{ti}_{ci}"] = st2.n

                    def s_z(s, ti=ti, ci=ci):
                        st2.wait(s, M[f"p1_smm_{ti}_{ci}"])
                        kprev = (f"p1_spx_{ti}_{ci - 2}" if ci >= 2 else
                                 (f"p1_spx_{ti - 1}_{ci + 2}" if ti > 0 else None))
                        if kprev is not None:
                            sv.wait(s, M[kprev])
                        zv = z_sp[ci % 2].rearrange("p (a b c) -> p a b c", a=8, b=2)
                        s.mul(out=zv[:, :, 0, :],
                              in_=ps_sA[ci % 2][:, 0:256].rearrange("p (a c) -> p a c", a=8),
                              mul=0.125)
                        ss.inc(s.mul(out=zv[:, :, 1, :],
                                     in_=ps_sB[ci % 2][:, 0:256].rearrange("p (a c) -> p a c", a=8),
                                     mul=0.125))
                    S_(s_z)
                    M[f"p1_zev_{ti}_{ci}"] = ss.n
                    if ti == 0 and ci == 0 and dump("d_z", [z_sp[0]]):
                        return nc

                    def v_sp(v, ti=ti, ci=ci):
                        ss.wait(v, M[f"p1_zev_{ti}_{ci}"])
                        kprev = (f"p1_atr_{ti}_{ci - 2}" if ci >= 2 else
                                 (f"p1_atr_{ti - 1}_{ci + 2}" if ti > 0 else None))
                        if kprev is not None:
                            st2.wait(v, M[kprev])
                        z = z_sp[ci % 2]
                        v.tensor_copy(out=srt, in_=z[:])
                        for k, (off, dims) in SORT_STEPS:
                            lo = carve(srt, off, dims)
                            hi = carve(srt, off + k, dims)
                            tmp = carve(srtmp, off, dims)
                            v.tensor_tensor(out=tmp, in0=lo, in1=hi, op=AO.min)
                            v.tensor_tensor(out=lo, in0=lo, in1=hi, op=AO.max)
                            v.tensor_copy(out=hi, in_=tmp)
                        for h in range(16):
                            v.tensor_tensor_scan(out=cum[:, 32 * h:32 * h + 32],
                                                 data0=srt[:, 32 * h:32 * h + 32],
                                                 data1=srt[:, 32 * h:32 * h + 32],
                                                 initial=0.0, op0=AO.add, op1=AO.bypass)
                        pv.inc(v.last)
                        pv.wait(v)
                        v.tensor_tensor(out=sup, in0=kvec, in1=srt, op=AO.mult)
                        v.tensor_scalar(out=sup, in0=sup, scalar1=1.0, scalar2=None,
                                        op0=AO.add)
                        v.tensor_tensor(out=sup, in0=sup, in1=cum, op=AO.is_gt)
                        v.reduce_sum(out=ksup[:, :],
                                     in_=sup.rearrange("p (h s) -> p h s", h=16), axis=AX)
                        v.tensor_tensor(out=srtmp, in0=sup, in1=srt, op=AO.mult)
                        pv.inc(v.reduce_sum(out=csel[:, :],
                                            in_=srtmp.rearrange("p (h s) -> p h s", h=16),
                                            axis=AX))
                        pv.wait(v)
                        v.tensor_scalar(out=csel[:, :], in0=csel[:, :], scalar1=1.0,
                                        scalar2=None, op0=AO.subtract)
                        pv.inc(v.reciprocal(out=ksup[:, :], in_=ksup[:, :]))
                        pv.wait(v)
                        pv.inc(v.tensor_tensor(out=tau[:, :], in0=csel[:, :], in1=ksup[:, :],
                                               op=AO.mult))
                        pv.wait(v)
                        for h in range(16):
                            v.tensor_scalar(out=a_bf[ci % 2][:, 32 * h:32 * h + 32],
                                            in0=z[:, 32 * h:32 * h + 32],
                                            scalar1=tau[:, h:h + 1], scalar2=0.0,
                                            op0=AO.subtract, op1=AO.max)
                        sv.inc(v.last)
                    V_(v_sp)
                    M[f"p1_spx_{ti}_{ci}"] = sv.n
                    if ti == 0 and ci == 0 and dump("d_A", [a_bf[0][:]]):
                        return nc

                    def t_at(t, ti=ti, ci=ci):
                        sv.wait(t, M[f"p1_spx_{ti}_{ci}"])
                        ss.wait(t, M[f"p1_aev_{ti}_{ci - 1}"] if ci > 0
                                else M[f"p1_aTev_{ti}_3"])
                        for kpair in range(8):
                            gg, hi = kpair % 4, kpair // 4
                            col = 128 * gg + 512 * hi
                            h0 = 4 * gg + 2 * hi
                            t.transpose(ps_tr[0:64, col:col + 128],
                                        in_=a_bf[ci % 2][:, 32 * h0:32 * h0 + 64],
                                        identity=ident[:])
                        st2.inc(t.last)
                    T_(t_at)
                    M[f"p1_atr_{ti}_{ci}"] = st2.n

                    def s_ae(s, ti=ti, ci=ci):
                        st2.wait(s, M[f"p1_atr_{ti}_{ci}"])
                        s.copy(out=at_lo[:, :, 128 * ci:128 * (ci + 1)],
                               in_=ps_tr[0:64, 0:512].rearrange("p (c n) -> p c n", c=4))
                        s.copy(out=at_hi[:, :, 128 * ci:128 * (ci + 1)],
                               in_=ps_tr[0:64, 512:1024].rearrange("p (c n) -> p c n", c=4))
                        ss.inc(s.last)
                    S_(s_ae)
                    M[f"p1_aev_{ti}_{ci}"] = ss.n

                for ci, c in enumerate(cs):
                    def t_avn(t, ti=ti, ci=ci):
                        ss.wait(t, M[f"p1_aev_{ti}_{ci}"])
                        kprev = (f"p1_oev_{ti}_{ci - 1}" if ci > 0 else
                                 (f"p1_oev_{ti - 1}_3" if ti > 0 else None))
                        if kprev is not None:
                            ss.wait(t, M[kprev])
                        for h in range(16):
                            m = h % 4
                            srcT = at_lo if m < 2 else at_hi
                            rr = 32 * (m % 2)
                            ph = PS_G[h % 2]
                            t.matmul(ph[:, 64 * (h // 2):64 * (h // 2) + 64],
                                     lhsT=srcT[rr:rr + 32, h // 4,
                                               128 * ci:128 * (ci + 1)].opt(),
                                     rhs=v_o[rr:rr + 32, 64 * h:64 * h + 64],
                                     start=True, stop=True)
                        st2.inc(t.last)
                    T_(t_avn)
                    M[f"p1_avn_{ti}_{ci}"] = st2.n

                    def s_oe(s, ti=ti, ci=ci, c=c):
                        st2.wait(s, M[f"p1_avn_{ti}_{ci}"])
                        if ti + ci > 0:
                            ld.wait(s, M.get(f"p1_ogw_{ti}_{ci - 1}",
                                             M.get(f"p1_ogw_{ti - 1}_3", 0)))
                        sl = vpool[:, 8192 + 1024 * (c % 2):8192 + 1024 * (c % 2) + 1024]
                        slv = sl.rearrange("p (a b c) -> p a b c", a=8, b=2)
                        s.copy(out=slv[:, :, 0, :],
                               in_=ps_g1[:].rearrange("p (a c) -> p a c", a=8))
                        s.copy(out=slv[:, :, 1, :],
                               in_=ps_g2[:].rearrange("p (a c) -> p a c", a=8))
                        ss.inc(s.last)
                    S_(s_oe)
                    M[f"p1_oev_{ti}_{ci}"] = ss.n

                    def y_og(y, ti=ti, ci=ci, c=c):
                        ss.wait(y, M[f"p1_oev_{ti}_{ci}"])
                        sl = vpool[:, 8192 + 1024 * (c % 2):8192 + 1024 * (c % 2) + 1024]
                        ld.dma(y.dma_start(
                            out=ago_in.rearrange("(c p) d -> p c d", p=128)[:, c, :],
                            in_=sl))
                    Y_(y_og)
                    M[f"p1_ogw_{ti}_{ci}"] = ld.n

            def g_ago(g):
                ld.wait(g, M["p1_ogw_3_3"])
                cc.inc(g.collective_compute("AllGather", AO.bypass, replica_groups=RG,
                                            ins=[ago_in.opt()], outs=[ago_out.opt()]))
            G_(g_ago)
            M["ago"] = cc.n

            ago_h = ago_out.tensor
    
            p1b_ld0, p1b_st0, p1b_ss0, p1b_sv0 = ld.n, st2.n, ss.n, sv.n
            T1b = lambda c, k: p1b_st0 + 2 * c + k
            S1b = lambda c, k: p1b_ss0 + 3 * c + k
            V1b = lambda c, k: p1b_sv0 + 2 * c + k

            def y_1b(y):
                for c in range(NCH):
                    if c == 0:
                        cc.wait(y, M["ago"])
                    if c >= 2:
                        st2.wait(y, T1b(c - 2, 1))
                    y.reg_load(soff_reg, soff_s[0:1, c:c + 1])
                    in_ap = AP(ago_h, soff_reg, [[16384, 128], [1024, 16], [1, 64]])
                    ld.dma(y.dma_start(
                        out=hT_t[c % 2][:, 0:1024].rearrange("p (a b) -> p a b", a=16),
                        in_=in_ap))
            Y_(y_1b)
            assert ld.n == p1b_ld0 + 16 * NCH

            def t_1b(t):
                for c in range(NCH):
                    ld.wait(t, p1b_ld0 + 16 * (c + 1))
                    if c == 0:
                        ss.wait(t, M["p1_oev_3_3"])
                    else:
                        ss.wait(t, S1b(c - 1, 1))
                    for j in range(8):
                        t.transpose(ps_tr[:, 128 * j:128 * (j + 1)],
                                    in_=hT_t[c % 2][:, 128 * j:128 * (j + 1)],
                                    identity=ident[:])
                    st2.inc(t.last)
                    ss.wait(t, S1b(c, 1))
                    if c == 0:
                        wld.wait(t, M["p0_woo"])
                    if c > 0:
                        sv.wait(t, V1b(c - 1, 2))
                    for nh in range(2):
                        ph = PS_G[nh]
                        for j in range(8):
                            t.matmul(ph[:, :],
                                     lhsT=ATP[:, j, 128 * (c % 2):128 * (c % 2) + 128].opt(),
                                     rhs=WOO[:, j, 512 * nh:512 * (nh + 1)],
                                     start=(j == 0), stop=False)
                        t.matmul(ph[:, :], lhsT=ones_bf[0:1, 0:128],
                                 rhs=bo_r[0:1, 512 * nh:512 * (nh + 1)],
                                 start=False, stop=True)
                    st2.inc(t.last)
            T_(t_1b)
            assert st2.n == p1b_st0 + 2 * NCH

            def s_1b(s):
                for c in range(NCH):
                    st2.wait(s, T1b(c, 1))
                    if c >= 2:
                        st2.wait(s, T1b(c - 2, 2))
                    ss.inc(s.copy(out=ATP[:, :, 128 * (c % 2):128 * (c % 2) + 128],
                                  in_=ps_tr[:].rearrange("p (c n) -> p c n", c=8)))
                    st2.wait(s, T1b(c, 2))
                    s.activation(out=scr1k[:, 0:512], in_=ps_g1[:], func=AF.Square,
                                 accum_out=scr16[:, 0:1])
                    ss.inc(s.activation(out=scr1k[:, 0:512], in_=ps_g2[:], func=AF.Square,
                                        accum_out=scr16[:, 1:2]))
                    sv.wait(s, V1b(c, 1))
                    pw.inc(s.activation(out=scr16[:, 7:8], in_=scr16[:, 5:6], func=AF.Ln))
                    pw.wait(s)
                    ss.inc(s.activation(out=scr16[:, 7:8], in_=scr16[:, 7:8],
                                        func=AF.Exp, scale=-0.5))
            S_(s_1b)
            assert ss.n == p1b_ss0 + 3 * NCH

            def v_1b(v):
                for c in range(NCH):
                    ss.wait(v, S1b(c, 2))
                    v.reduce_sum(out=scr16[:, 2:3], in_=ps_g1[:], axis=AX)
                    pv.inc(v.reduce_sum(out=scr16[:, 3:4], in_=ps_g2[:], axis=AX))
                    pv.wait(v)
                    pv.inc(v.tensor_tensor(out=scr16[:, 4:5], in0=scr16[:, 2:3],
                                           in1=scr16[:, 3:4], op=AO.add))
                    pv.wait(v)
                    pv.inc(v.tensor_tensor(out=scr16[:, 5:6], in0=scr16[:, 0:1],
                                           in1=scr16[:, 1:2], op=AO.add))
                    pv.wait(v)
                    v.tensor_scalar(out=scr16[:, 4:5], in0=scr16[:, 4:5],
                                    scalar1=1.0 / D, scalar2=None, op0=AO.mult)
                    pv.inc(v.tensor_scalar(out=scr16[:, 5:6], in0=scr16[:, 5:6],
                                           scalar1=1.0 / D, scalar2=None, op0=AO.mult))
                    pv.wait(v)
                    pv.inc(v.tensor_tensor(out=scr16[:, 6:7], in0=scr16[:, 4:5],
                                           in1=scr16[:, 4:5], op=AO.mult))
                    pv.wait(v)
                    pv.inc(v.tensor_tensor(out=scr16[:, 5:6], in0=scr16[:, 5:6],
                                           in1=scr16[:, 6:7], op=AO.subtract))
                    pv.wait(v)
                    sv.inc(v.tensor_scalar(out=scr16[:, 5:6], in0=scr16[:, 5:6],
                                           scalar1=EPS, scalar2=None, op0=AO.add))
                    ss.wait(v, S1b(c, 3))
                    pv.inc(v.engine_nop())
                    pv.wait(v)
                    for nh in range(2):
                        v.tensor_scalar(out=scr1k2[:, 512 * nh:512 * (nh + 1)],
                                        in0=PS_G[nh][:, :], scalar1=scr16[:, 4:5],
                                        scalar2=scr16[:, 7:8],
                                        op0=AO.subtract, op1=AO.mult)
                    sv.inc(v.tensor_tensor(out=A(c), in0=A(c), in1=scr1k2[:],
                                           op=AO.add))
            V_(v_1b)
            assert sv.n == p1b_sv0 + 2 * NCH
            for c in range(NCH):
                M[f"p1_x1_{c}"] = V1b(c, 2)

            if dump("d_x1c", [A(0), scr1k2[:], scr16[:, 0:8]]):
                return nc

            if dump("p1", [bufA[:]]):
                return nc

            p15_sv0 = sv.n
            p15_ss0 = ss.n
            p15_st0 = st2.n
            p15_ld0 = ld.n
            VS = lambda c, k: p15_sv0 + 4 * c + k
            SS = lambda c, k: p15_ss0 + 4 * c + k
            TS_ = lambda c: p15_st0 + c + 1
            LD = lambda c, k: p15_ld0 + 32 * c + 16 * k

            def y_15(y):
                for c in range(NCH):
                    sv.wait(y, M[f"p1_x1_{c}"])
                    ld.dma(y.dma_start(
                        out=x1_d.rearrange("(c p) d -> p c d", p=128)[:, c, :], in_=A(c)))
                    ss.wait(y, SS(c, 4))
                    ld.dma(y.dma_start(
                        out=ag1_in.rearrange("(k j p) t -> p k j t", p=128, k=4)
                        [:, c // 4, :, 128 * (c % 4):128 * (c % 4) + 128],
                        in_=hT_t[c % 2][:, 0:1024].rearrange("p (j n) -> p j n", j=8)))
            Y_(y_15)
            assert ld.n == p15_ld0 + 32 * NCH
            for c in range(NCH):
                M[f"p15_hw_{c}"] = LD(c, 2)
            M["p15_hw_15"] = ld.n

            def v_15(v):
                for c in range(NCH):
                    sv.inc(v.reduce_sum(out=sumv[:, c:c + 1], in_=A(c), axis=AX))
                    ss.wait(v, SS(c, 1))
                    sv.wait(v, VS(c, 1))
                    sl = slice(c, c + 1)
                    v.tensor_scalar(out=m1_v[:, sl], in0=sumv[:, sl], scalar1=1.0 / D,
                                    scalar2=None, op0=AO.mult)
                    pv.inc(v.tensor_scalar(out=var_v[:, sl], in0=sumsq[:, sl],
                                           scalar1=1.0 / D, scalar2=None, op0=AO.mult))
                    pv.wait(v)
                    pv.inc(v.tensor_tensor(out=scr16[:, 0:1], in0=m1_v[:, sl],
                                           in1=m1_v[:, sl], op=AO.mult))
                    pv.wait(v)
                    pv.inc(v.tensor_tensor(out=var_v[:, sl], in0=var_v[:, sl],
                                           in1=scr16[:, 0:1], op=AO.subtract))
                    pv.wait(v)
                    sv.inc(v.tensor_scalar(out=var_v[:, sl], in0=var_v[:, sl], scalar1=EPS,
                                           scalar2=None, op0=AO.add))
                    ss.wait(v, SS(c, 2))
                    v.tensor_tensor(out=scr16[:, 2:3], in0=r1_v[:, sl], in1=r1_v[:, sl],
                                    op=AO.mult)
                    pv.inc(v.tensor_scalar(out=scr16[:, 3:4], in0=var_v[:, sl], scalar1=EPS,
                                           scalar2=None, op0=AO.subtract))
                    pv.wait(v)
                    pv.inc(v.tensor_tensor(out=scr16[:, 2:3], in0=scr16[:, 2:3],
                                           in1=scr16[:, 3:4], op=AO.mult))
                    pv.wait(v)
                    sv.inc(v.tensor_scalar(out=scr16[:, 2:3], in0=scr16[:, 2:3], scalar1=EPS,
                                           scalar2=None, op0=AO.add))
                    ss.wait(v, SS(c, 3))
                    ld.wait(v, LD(c, 1))
                    if c >= 2:
                        ld.wait(v, LD(c - 2, 2))
                        st2.wait(v, TS_(c - 2))
                    pv.inc(v.tensor_tensor(out=scr16[:, 4:5], in0=r1_v[:, c:c + 1],
                                           in1=c2_v[:, c:c + 1], op=AO.mult))
                    pv.wait(v)
                    sv.inc(v.tensor_scalar(out=a_nat[c % 2], in0=A(c),
                                           scalar1=m1_v[:, c:c + 1],
                                           scalar2=scr16[:, 4:5],
                                           op0=AO.subtract, op1=AO.mult))
                    v.tensor_scalar(out=A(c), in0=A(c), scalar1=m1_v[:, c:c + 1],
                                    scalar2=r1_v[:, c:c + 1], op0=AO.subtract, op1=AO.mult)
            V_(v_15)
            assert sv.n == p15_sv0 + 4 * NCH
            M["p15_vend"] = sv.n

            def s_15(s):
                for c in range(NCH):
                    if c == 0:
                        sv.wait(s, M["p1_x1_15"])
                    ss.inc(s.activation(out=scr1k[:], in_=A(c), func=AF.Square,
                                        accum_out=sumsq[:, c:c + 1]))
                    sv.wait(s, VS(c, 2))
                    sl = slice(c, c + 1)
                    pw.inc(s.activation(out=scr16[:, 1:2], in_=var_v[:, sl], func=AF.Ln))
                    pw.wait(s)
                    s.activation(out=r1_v[:, sl], in_=scr16[:, 1:2], func=AF.Exp, scale=-0.5)
                    ss.inc(s.activation(out=sd_v[:, sl], in_=scr16[:, 1:2], func=AF.Exp,
                                        scale=0.5))
                    sv.wait(s, VS(c, 3))
                    pw.inc(s.activation(out=scr16[:, 5:6], in_=scr16[:, 2:3], func=AF.Ln))
                    pw.wait(s)
                    ss.inc(s.activation(out=c2_v[:, c:c + 1], in_=scr16[:, 5:6], func=AF.Exp,
                                        scale=-0.5))
                    st2.wait(s, TS_(c))
                    if c >= 2:
                        ld.wait(s, LD(c - 2, 2))
                    ss.inc(s.copy(out=hT_t[c % 2][:, 0:1024], in_=ps_tr[:]))
            S_(s_15)
            assert ss.n == p15_ss0 + 4 * NCH
            M["p15_send"] = ss.n

            def t_15(t):
                for c in range(NCH):
                    sv.wait(t, VS(c, 4))
                    if c == 0:
                        ss.wait(t, M["p1_oev_3_3"])
                    else:
                        ss.wait(t, SS(c - 1, 4))
                    for j in range(8):
                        t.transpose(ps_tr[:, 128 * j:128 * (j + 1)],
                                    in_=a_nat[c % 2][:, 128 * j:128 * (j + 1)],
                                    identity=ident[:])
                    st2.inc(t.last)
            T_(t_15)
            assert st2.n == p15_st0 + NCH

            if dump("p15", [bufA[:]]):
                return nc

            def g_mask(g):
                sv.wait(g, M["p1_spx_3_3"])
                wld.dma(g.dma_start(out=masks, in_=mask_d[:, :]))
                ss.wait(g, M["p15_send"])
                sv.wait(g, M["p15_vend"])
                wld.dma(g.dma_start(out=wvi_s,
                                    in_=wvi_d.rearrange("(c p) n -> p c n", p=128)))
            G_(g_mask)
            M["wvi_ld"] = wld.n

            ag1_c0 = cc.n

            def g_a1(g):
                for ks in range(4):
                    ld.wait(g, M[f"p15_hw_{4 * ks + 3}"])
                    cc.inc(g.collective_compute(
                        "AllGather", AO.bypass, replica_groups=RG,
                        ins=[ag1_in[1024 * ks:1024 * (ks + 1), :].opt()],
                        outs=[ag1_out[4096 * ks:4096 * (ks + 1), :].opt()]))
            G_(g_a1)
            M["ag1"] = cc.n

            ORD3A = [4 * r + k for k in range(4) for r in range(4)]
            for oi, tt in enumerate(ORD3A):
                def y_ht(y, oi=oi, tt=tt):
                    k, r = tt % 4, tt // 4
                    if oi % 4 == 0:
                        cc.wait(y, ag1_c0 + k + 1)
                    if oi >= 2:
                        st2.wait(y, M[f"p3a_v_{ORD3A[oi - 2]}"])
                    ld.dma(y.dma_start(
                        out=HTT[oi % 2],
                        in_=ag1_out.rearrange("(k r j p) t -> p k r j t", k=4, r=4, j=8)
                        [:, k, r, :, :]))
                Y_(y_ht)
                M[f"p3a_ld_{tt}"] = ld.n

                def t_qkv(t, oi=oi, tt=tt):
                    ld.wait(t, M[f"p3a_ld_{tt}"])
                    if oi == 0:
                        wld.wait(t, M["wvi_ld"])
                    ht = HTT[oi % 2]
                    if oi > 0:
                        sv.wait(t, M[f"p3a_qke_{ORD3A[oi - 1]}"])
                    for wi, w in enumerate([wqi_s, wki_s]):
                        phs = PS_G if wi == 0 else ps_sB
                        for cj in range(2):
                            for dc in range(8):
                                t.matmul(phs[cj][:, :], lhsT=w[:, dc, 128 * cj:128 * (cj + 1)],
                                         rhs=ht[:, dc, :].opt(), start=(dc == 0), stop=(dc == 7))
                        st2.inc(t.last)
                    ve_base = ss.n
                    for tc in range(4):
                        if tc >= 2:
                            ss.wait(t, ve_base + (tc - 1))
                        for dc in range(8):
                            t.matmul(ps_sA[tc % 2][:, 0:256],
                                     lhsT=ht[:, dc, 128 * tc:128 * (tc + 1)].opt(),
                                     rhs=wvi_s[:, dc, :], start=(dc == 0), stop=(dc == 7))
                        st2.inc(t.last)
                T_(t_qkv)
                M[f"p3a_t_{tt}"] = st2.n
                M[f"p3a_v_{tt}"] = st2.n

                def v_qk(v, oi=oi, tt=tt):
                    if oi == 0:
                        vi = vpool[:].rearrange("p (c m n) -> p c m n", c=64, m=4)
                        v.memset(vi[:, :, :, 64:65], 1.0)
                    base = M[f"p3a_t_{tt}"] - 6
                    for wi, dst in [(0, QTI), (1, KTI)]:
                        st2.wait(v, base + wi + 1)
                        phs = PS_G if wi == 0 else ps_sB
                        for cj in range(2):
                            v.tensor_copy(out=dst[:, cj, 512 * tt:512 * (tt + 1)].opt(),
                                          in_=phs[cj][:, :])
                        sv.inc(v.last)
                V_(v_qk)
                M[f"p3a_qke_{tt}"] = sv.n

                def s_v(s, tt=tt):
                    base = M[f"p3a_t_{tt}"] - 4
                    for tc in range(4):
                        st2.wait(s, base + tc + 1)
                        kc = 4 * tt + tc
                        for m in range(4):
                            s.copy(out=V_I[:, kc, 65 * m:65 * m + 64].opt(),
                                   in_=ps_sA[tc % 2][:, 64 * m:64 * m + 64])
                        ss.inc(s.last)
                S_(s_v)
                M[f"p3a_ve_{tt}"] = ss.n
                if oi == 0 and dump("d_h", [hT_t[0][:, 0:4096], ag1_out[0:128, :]]):
                    return nc

            if dump("p3q", [wpool[:, 0:4096], wpool[:, 2 * T_FULL:2 * T_FULL + 4096],
                            vpool[:, 0:4096], vpool[:, 4096:8192]]):
                return nc

            p3b_st0, p3b_ss0, p3b_sv0, p3b_ld0 = st2.n, ss.n, sv.n, ld.n
            BL3 = []
            _st, _ss, _sv, _ld = p3b_st0, p3b_ss0, p3b_sv0, p3b_ld0
            for qi in range(16):
                for pj in range(2):
                    nkc = 4 * qi + 4
                    BL3.append((qi, pj, nkc, _st, _ss, _sv))
                    M[f"p3b_avend_{qi}_{pj}"] = _st + 2 * nkc
                    M[f"p3b_expend_{qi}_{pj}"] = _ss + nkc
                    M[f"p3b_nrm2_{qi}_{pj}"] = _sv + 8
                    _st += 2 * nkc + 2
                    _ss += nkc + 4
                    _sv += 8
                    _ld += 16
                    M[f"p3b_ow_{qi}_{pj}"] = _ld

            def t3b(t):
                for (qi, pj, nkc, st2_0, ss_0, sv_0) in BL3:
                    mA, mB = 2 * pj, 2 * pj + 1
                    first = (qi == 0 and pj == 0)
                    if first:
                        sv.wait(t, M["p3a_qke_15"])
                        ss.wait(t, M["p3a_ve_15"])
                    for kc in range(nkc):
                        par = kc % 2
                        if kc >= 2:
                            ss.wait(t, ss_0 + kc - 1)
                        elif not first:
                            ss.wait(t, M[f"p3b_expend_{qi}_{pj - 1}" if pj == 1
                                         else f"p3b_expend_{qi - 1}_1"])
                        t.matmul(ps_sA[par][:],
                                 lhsT=KTI[0:64, pj, 128 * kc:128 * (kc + 1)].opt(),
                                 rhs=QTI[0:64, pj, 512 * qi:512 * (qi + 1)].opt(),
                                 start=True, stop=True)
                        st2.inc(t.matmul(ps_sB[par][:],
                                         lhsT=KTI[64:128, pj, 128 * kc:128 * (kc + 1)].opt(),
                                         rhs=QTI[64:128, pj, 512 * qi:512 * (qi + 1)].opt(),
                                         start=True, stop=True, tile_position=(64, 0)))
                        kav = kc - 1
                        if kav >= 0:
                            parv = kav % 2
                            diag = kav - 4 * qi
                            if 0 <= diag < 4:
                                sv.wait(t, sv_0 + diag + 1)
                            else:
                                ss.wait(t, ss_0 + kav + 1)
                            if kav == 0:
                                k_no = M.get(f"p3b_nrm2_{qi}_{pj - 1}" if pj == 1
                                             else (f"p3b_nrm2_{qi - 1}_1" if qi > 0 else None))
                                if k_no is not None:
                                    sv.wait(t, k_no)
                            t.matmul(ps_oA[0:65, :],
                                     lhsT=V_I[:, kav, 65 * mA:65 * mA + 65].opt(),
                                     rhs=e_bf[2 * parv][:], start=(kav == 0),
                                     stop=(kav == nkc - 1))
                            st2.inc(t.matmul(ps_oB[0:65, :],
                                             lhsT=V_I[:, kav, 65 * mB:65 * mB + 65].opt(),
                                             rhs=e_bf[2 * parv + 1][:], start=(kav == 0),
                                             stop=(kav == nkc - 1)))
                    kav = nkc - 1
                    parv = kav % 2
                    sv.wait(t, sv_0 + 4)
                    t.matmul(ps_oA[0:65, :], lhsT=V_I[:, kav, 65 * mA:65 * mA + 65].opt(),
                             rhs=e_bf[2 * parv][:], start=(kav == 0), stop=True)
                    st2.inc(t.matmul(ps_oB[0:65, :],
                                     lhsT=V_I[:, kav, 65 * mB:65 * mB + 65].opt(),
                                     rhs=e_bf[2 * parv + 1][:], start=(kav == 0), stop=True))
                    for hh in range(2):
                        sv.wait(t, sv_0 + 4 + 2 * hh + 1)
                        if hh == 1:
                            ss.wait(t, ss_0 + nkc + 2)
                        st2.inc(t.matmul(ps_g1[0:64, :], lhsT=ones_f,
                                         rhs=rec_s, start=True, stop=True))
            T_(t3b)
            assert st2.n == _st

            def s3b(s):
                for (qi, pj, nkc, st2_0, ss_0, sv_0) in BL3:
                    for kc in range(nkc):
                        par = kc % 2
                        st2.wait(s, st2_0 + (1 if kc == 0 else 2 * kc))
                        s.activation(out=e_bf[2 * par][:], in_=ps_sA[par][:],
                                     func=AF.Exp, scale=0.125)
                        ss.inc(s.activation(out=e_bf[2 * par + 1][:], in_=ps_sB[par][:],
                                            func=AF.Exp, scale=0.125))
                    for hh in range(2):
                        ps_o = ps_oA if hh == 0 else ps_oB
                        st2.wait(s, st2_0 + 2 * nkc)
                        if hh == 1:
                            sv.wait(s, sv_0 + 4 + 2)
                        ss.inc(s.copy(out=den_s, in_=ps_o[64:65, :]))
                        st2.wait(s, st2_0 + 2 * nkc + hh + 1)
                        if hh == 1:
                            sv.wait(s, sv_0 + 4 + 3)
                        ss.inc(s.copy(out=recb, in_=ps_g1[0:64, :]))
            S_(s3b)
            assert ss.n == _ss

            def v3b(v):
                for (qi, pj, nkc, st2_0, ss_0, sv_0) in BL3:
                    for d2 in range(4):
                        kc = 4 * qi + d2
                        ss.wait(v, ss_0 + kc + 1)
                        v.tensor_tensor(out=e_bf[2 * (kc % 2)][:], in0=e_bf[2 * (kc % 2)][:],
                                        in1=masks[:, 512 * d2:512 * (d2 + 1)], op=AO.mult)
                        sv.inc(v.tensor_tensor(out=e_bf[2 * (kc % 2) + 1][:],
                                               in0=e_bf[2 * (kc % 2) + 1][:],
                                               in1=masks[:, 512 * d2:512 * (d2 + 1)],
                                               op=AO.mult))
                    for hh in range(2):
                        ps_o = ps_oA if hh == 0 else ps_oB
                        if qi == 0 and pj == 0 and hh == 0:
                            v.memset(ones_f, 1.0)
                        ss.wait(v, ss_0 + nkc + 2 * hh + 1)
                        sv.inc(v.reciprocal(out=rec_s, in_=den_s))
                        ss.wait(v, ss_0 + nkc + 2 * hh + 2)
                        if hh == 0 and not (qi == 0 and pj == 0):
                            ld.wait(v, M[f"p3b_ow_{qi}_{pj - 1}" if pj == 1
                                         else f"p3b_ow_{qi - 1}_1"])
                        sv.inc(v.tensor_tensor(out=ob_s[64 * hh:64 * hh + 64, :],
                                               in0=ps_o[0:64, :], in1=recb,
                                               op=AO.mult))
            V_(v3b)
            assert sv.n == _sv

            def y3b(y):
                for (qi, pj, nkc, st2_0, ss_0, sv_0) in BL3:
                    sv.wait(y, M[f"p3b_nrm2_{qi}_{pj}"])
                    ld.dma(y.dma_start(
                        out=ag2_in[256 * (qi // 4) + 128 * pj:256 * (qi // 4) + 128 * (pj + 1),
                                   512 * (qi % 4):512 * (qi % 4 + 1)],
                        in_=ob_s[:]))
            Y_(y3b)
            assert ld.n == _ld

            if dump("p3", []):
                return nc

            def _a2_split(g, ms):
                ld.wait(g, M[f"p3b_ow_{4 * ms + 3}_1"])
                cc.inc(g.collective_compute(
                    "AllGather", AO.bypass, replica_groups=RG,
                    ins=[ag2_in[256 * ms:256 * (ms + 1), :].opt()],
                    outs=[ag2_out[1024 * ms:1024 * (ms + 1), :].opt()]))

            def g_a2w(g):
                for ms in range(3):
                    _a2_split(g, ms)
                st2.wait(g, M["p3b_avend_15_1"])
                wld.dma(g.dma_start(out=woi_s, in_=woi_d.rearrange("(c p) n -> p c n", p=128)))
                _a2_split(g, 3)
            G_(g_a2w)
            M["woi_ld"] = wld.n - 16
            M["ag2"] = cc.n

            p4_sv0, p4_ss0, p4_st0, p4_ld0, p4_f0 = sv.n, ss.n, st2.n, ld.n, fc2.n
            P4B = lambda c: hT_t[c % 2][:, 1024 * ((c // 2) % 2):
                                        1024 * ((c // 2) % 2) + 1024]

            def _p4_load(eng, reg, c):
                if c < 2:
                    cc.wait(eng, M["ag2"])
                if c >= 4:
                    st2.wait(eng, T4(c - 4, 1))
                eng.reg_load(reg, toff_s[0:1, c:c + 1])
                in_ap = AP(ag2_out.tensor, reg, [[2048, 128], [262144, 8], [1, 128]])
                ctr = ld if c % 2 == 0 else fc2
                ctr.dma(eng.dma_start(
                    out=P4B(c).rearrange("p (a b) -> p a b", a=8), in_=in_ap))
            V4 = lambda c, k: p4_sv0 + 5 * c + k
            S4 = lambda c, k: p4_ss0 + 4 * c + k
            T4 = lambda c, k: p4_st0 + 3 * c + k
            L4 = lambda c: (p4_ld0 if c % 2 == 0 else p4_f0) + 16 * (c // 2 + 1)

            def y_p4(y):
                for c in range(1, NCH, 2):
                    _p4_load(y, soff_reg, c)
            Y_(y_p4)
            assert fc2.n == p4_f0 + 16 * (NCH // 2)

            def s_p4(s):
                _p4_load(s, toff_reg, 0)
                _p4_load(s, toff_reg, 2)
                for c in range(NCH):
                    sv.wait(s, V4(c, 1))
                    ss.inc(s.activation(out=scr1k[:], in_=A(c), func=AF.Square,
                                        accum_out=scr16[:, 0:1]))
                    sv.wait(s, V4(c, 2))
                    pw.inc(s.activation(out=scr16[:, 2:3], in_=scr16[:, 1:2], func=AF.Ln))
                    pw.wait(s)
                    ss.inc(s.activation(out=scr16[:, 3:4], in_=scr16[:, 2:3],
                                        func=AF.Exp, scale=-0.5))
                    st2.wait(s, T4(c, 2))
                    ss.inc(s.copy(out=RT[:, :, 128 * c:128 * (c + 1)],
                                  in_=ps_tr[:].rearrange("p (c n) -> p c n", c=8)))
                    sv.wait(s, V4(c, 4))
                    ss.inc(s.activation(out=pr8[:], in_=lg8[:], func=AF.Exp))
                    if c % 2 == 0 and c + 4 < NCH:
                        _p4_load(s, toff_reg, c + 4)
            S_(s_p4)
            assert ss.n == p4_ss0 + 4 * NCH and ld.n == p4_ld0 + 16 * (NCH // 2)

            def t_p4(t):
                for c in range(NCH):
                    (ld if c % 2 == 0 else fc2).wait(t, L4(c))
                    if c == 0:
                        wld.wait(t, M["woi_ld"])
                    if c > 0:
                        sv.wait(t, V4(c - 1, 1))
                    for nh in range(2):
                        for j in range(8):
                            t.matmul(PS_G[nh][:, :],
                                     lhsT=P4B(c)[:, 128 * j:128 * (j + 1)],
                                     rhs=woi_s[:, j, 512 * nh:512 * (nh + 1)],
                                     start=(j == 0), stop=(j == 7))
                    st2.inc(t.last)
                    sv.wait(t, V4(c, 3))
                    if c > 0:
                        ss.wait(t, S4(c - 1, 3))
                    for j in range(8):
                        t.transpose(ps_tr[:, 128 * j:128 * (j + 1)],
                                    in_=r_bf[:, 128 * j:128 * (j + 1)], identity=ident[:])
                    st2.inc(t.last)
                    ss.wait(t, S4(c, 3))
                    if c > 0:
                        sv.wait(t, V4(c - 1, 4))
                    for dc in range(8):
                        t.matmul(ps_oA[:, 0:8], lhsT=RT[:, dc, 128 * c:128 * (c + 1)].opt(),
                                 rhs=rw_s[:, dc, :], start=(dc == 0), stop=(dc == 7))
                    st2.inc(t.last)
            T_(t_p4)
            assert st2.n == p4_st0 + 3 * NCH

            def v_p4(v):
                for c in range(NCH):
                    st2.wait(v, T4(c, 1))
                    for nh in range(2):
                        v.tensor_tensor(out=A(c)[:, 512 * nh:512 * (nh + 1)],
                                        in0=PS_G[nh][:, :],
                                        in1=A(c)[:, 512 * nh:512 * (nh + 1)], op=AO.add)
                    sv.inc(v.last)
                    ss.wait(v, S4(c, 1))
                    sv.inc(v.tensor_scalar(out=scr16[:, 1:2], in0=scr16[:, 0:1],
                                           scalar1=1.0 / D, scalar2=EPS,
                                           op0=AO.mult, op1=AO.add))
                    ss.wait(v, S4(c, 2))
                    sv.inc(v.tensor_scalar(out=r_bf, in0=A(c), scalar1=scr16[:, 3:4],
                                           scalar2=None, op0=AO.mult))
                    st2.wait(v, T4(c, 3))
                    pv.inc(v.tensor_copy(out=lg8[:], in_=ps_oA[:, 0:8]))
                    pv.wait(v)
                    pv.inc(v.reduce_max(out=tv16[:, 0:1], in_=lg8[:], axis=AX))
                    pv.wait(v)
                    sv.inc(v.tensor_scalar(out=lg8[:], in0=lg8[:], scalar1=tv16[:, 0:1],
                                           scalar2=None, op0=AO.subtract))
                    ss.wait(v, S4(c, 4))
                    pv.inc(v.reduce_sum(out=tv16[:, 1:2], in_=pr8[:], axis=AX))
                    pv.wait(v)
                    pv.inc(v.reciprocal(out=tv16[:, 2:3], in_=tv16[:, 1:2]))
                    pv.wait(v)
                    pv.inc(v.tensor_scalar(out=pr8[:], in0=pr8[:], scalar1=tv16[:, 2:3],
                                           scalar2=None, op0=AO.mult))
                    pv.wait(v)
                    pv.inc(v.reduce_max(out=tv16[:, 3:4], in_=pr8[:], axis=AX))
                    pv.wait(v)
                    pv.inc(v.tensor_scalar(out=msk8[:], in0=pr8[:], scalar1=tv16[:, 3:4],
                                           scalar2=None, op0=AO.is_ge))
                    pv.wait(v)
                    pv.inc(v.scalar_tensor_tensor(out=pm8[:], in0=msk8[:], scalar=-2.0,
                                                  in1=pr8[:], op0=AO.mult, op1=AO.add))
                    pv.wait(v)
                    pv.inc(v.reduce_max(out=tv16[:, 4:5], in_=pm8[:], axis=AX))
                    pv.wait(v)
                    pv.inc(v.tensor_tensor(out=tv16[:, 5:6], in0=tv16[:, 3:4],
                                           in1=tv16[:, 4:5], op=AO.add))
                    pv.wait(v)
                    pv.inc(v.reciprocal(out=tv16[:, 6:7], in_=tv16[:, 5:6]))
                    pv.wait(v)
                    v.tensor_tensor(out=tv16[:, 7:8], in0=tv16[:, 3:4], in1=tv16[:, 6:7],
                                    op=AO.mult)
                    pv.inc(v.tensor_tensor(out=tv16[:, 8:9], in0=tv16[:, 4:5],
                                           in1=tv16[:, 6:7], op=AO.mult))
                    pv.wait(v)
                    pv.inc(v.tensor_scalar(out=msk8b[:], in0=pm8[:], scalar1=tv16[:, 4:5],
                                           scalar2=None, op0=AO.is_ge))
                    pv.wait(v)
                    pv.inc(v.tensor_scalar(out=g_e[:, c, :].opt(), in0=msk8[:],
                                           scalar1=tv16[:, 7:8], scalar2=None, op0=AO.mult))
                    pv.wait(v)
                    sv.inc(v.scalar_tensor_tensor(out=g_e[:, c, :].opt(), in0=msk8b[:],
                                                  scalar=tv16[:, 8:9], in1=g_e[:, c, :].opt(),
                                                  op0=AO.mult, op1=AO.add))
            V_(v_p4)
            assert sv.n == p4_sv0 + 5 * NCH
            M["p4_rte_15"] = p4_ss0 + 4 * NCH
            M["p4_g_15"] = sv.n
            M["p4_wo_15"] = p4_st0 + 3 * (NCH - 1) + 1

            if dump("p4", [bufA[:]]):
                return nc
            if dump("d_g", [g_e[:, :, :].opt(), vpool[:, 0:2048], pr8[:], pm8[:], msk8[:], msk8b[:], tv16[:]]):
                return nc

            def _fc_slab(eng, ctr, idx):
                e, hb = idx // 2, idx % 2
                slot = idx % 2
                if idx == 0:
                    st2.wait(eng, M["p3b_avend_15_1"])
                elif idx == 1:
                    st2.wait(eng, M["p4_wo_15"])
                else:
                    st2.wait(eng, M.get(f"p5_cons_{idx - 2}", 0))
                ctr.dma(eng.dma_start(
                    out=FC1H[slot],
                    in_=fc1_d[D * e:D * (e + 1), 1024 * hb:1024 * (hb + 1)]
                    .rearrange("(c p) n -> p c n", p=128)))
                ctr.dma(eng.dma_start(
                    out=FC2H[slot],
                    in_=fc2_d[2048 * e + 1024 * hb:2048 * e + 1024 * (hb + 1), :]
                    .rearrange("(c p) n -> p c n", p=128)))

            def g_fc(g):
                for idx in range(16):
                    _fc_slab(g, fcl, idx)

            p5_cons_marks = {}
            fcl_marks = {}
            fcl_ctr = {}
            fcl_base = fcl.n
            for i in range(16):
                fcl_marks[i] = fcl_base + 32 * (i + 1)
                fcl_ctr[i] = fcl

            p5_st0, p5_ss0, p5_sv0 = st2.n, ss.n, sv.n
            BI = lambda e, hb, tb: 8 * (2 * e + hb) + tb
            for e in range(8):
                for hb in range(2):
                    idx = 2 * e + hb
                    for tb in range(8):
                        M[f"p5_ev_{e}_{hb}_{tb}"] = p5_sv0 + 9 * BI(e, hb, tb) + 9
                    p5_cons_marks[idx] = p5_st0 + 16 * 8 * (idx + 1)
                    M[f"p5_cons_{idx}"] = p5_cons_marks[idx]

            def t5_all(t):
                for e in range(8):
                    for hb in range(2):
                        idx = 2 * e + hb
                        slot = idx % 2
                        for tb in range(8):
                            i = BI(e, hb, tb)
                            st2_0 = p5_st0 + 16 * i
                            ss_0 = p5_ss0 + 8 * i
                            sv_0 = p5_sv0 + 9 * i
                            if idx == 0 and tb == 0:
                                fcl_ctr[0].wait(t, fcl_marks[0])
                                ss.wait(t, M["p4_rte_15"])
                                sv.wait(t, M["p4_g_15"])
                            elif tb == 0:
                                fcl_ctr[idx].wait(t, fcl_marks[idx])
                            for hc8 in range(8):
                                par = hc8 % 2
                                if hc8 >= 2:
                                    ss.wait(t, ss_0 + hc8 - 1)
                                for dc in range(8):
                                    t.matmul(ps_sA[par][:, 0:256],
                                             lhsT=FC1H[slot][:, dc, 128 * hc8:128 * (hc8 + 1)],
                                             rhs=RT[:, dc, 256 * tb:256 * (tb + 1)].opt(),
                                             start=(dc == 0), stop=(dc == 7))
                                st2.inc(t.last)
                                kg = hc8 - 1
                                if kg >= 0:
                                    parg = kg % 2
                                    sv.wait(t, sv_0 + kg + 1)
                                    if kg == 0:
                                        if tb > 0:
                                            k_ev = M.get(f"p5_ev_{e}_{hb}_{tb - 1}")
                                        elif hb == 1:
                                            k_ev = M.get(f"p5_ev_{e}_0_7")
                                        else:
                                            k_ev = M.get(f"p5_ev_{e - 1}_1_7" if e > 0 else None)
                                        if k_ev is not None:
                                            sv.wait(t, k_ev)
                                    accs = [[ps_oA, ps_oB], [ps_sB[0], ps_sB[1]]]
                                    for tc in range(2):
                                        for nh in range(2):
                                            t.matmul(accs[tc][nh][:, :],
                                                     lhsT=h1sq[:, 256 * parg + 128 * tc:
                                                               256 * parg + 128 * (tc + 1)],
                                                     rhs=FC2H[slot][:, kg, 512 * nh:512 * (nh + 1)],
                                                     start=(kg == 0), stop=False)
                                    st2.inc(t.last)
                            sv.wait(t, sv_0 + 8)
                            accs = [[ps_oA, ps_oB], [ps_sB[0], ps_sB[1]]]
                            for tc in range(2):
                                for nh in range(2):
                                    t.matmul(accs[tc][nh][:, :],
                                             lhsT=h1sq[:, 256 + 128 * tc:256 + 128 * (tc + 1)],
                                             rhs=FC2H[slot][:, 7, 512 * nh:512 * (nh + 1)],
                                             start=False, stop=True)
                            st2.inc(t.last)
            T_(t5_all)
            assert st2.n == p5_st0 + 16 * 128

            def s5_all(s):
                for i in range(128):
                    st2_0 = p5_st0 + 16 * i
                    ss_0 = p5_ss0 + 8 * i
                    sv_0 = p5_sv0 + 9 * i
                    for hc8 in range(8):
                        par = hc8 % 2
                        st2.wait(s, st2_0 + (1 if hc8 == 0 else 2 * hc8))
                        if hc8 >= 2:
                            sv.wait(s, sv_0 + hc8 - 1)
                        ss.inc(s.activation(out=scr1k[:, 256 * par:256 * (par + 1)],
                                            in_=ps_sA[par][:, 0:256], func=AF.Relu))
            S_(s5_all)
            assert ss.n == p5_ss0 + 8 * 128

            def v5_all(v):
                for e in range(8):
                    for hb in range(2):
                        for tb in range(8):
                            i = BI(e, hb, tb)
                            st2_0 = p5_st0 + 16 * i
                            ss_0 = p5_ss0 + 8 * i
                            for hc8 in range(8):
                                par = hc8 % 2
                                ss.wait(v, ss_0 + hc8 + 1)
                                if hc8 >= 2:
                                    st2.wait(v, st2_0 + 2 * (hc8 - 2) + 3)
                                sv.inc(v.tensor_tensor(
                                    out=h1sq[:, 256 * par:256 * (par + 1)],
                                    in0=scr1k[:, 256 * par:256 * (par + 1)],
                                    in1=scr1k[:, 256 * par:256 * (par + 1)], op=AO.mult))
                            st2.wait(v, st2_0 + 16)
                            accs = [[ps_oA, ps_oB], [ps_sB[0], ps_sB[1]]]
                            for tc in range(2):
                                ch = 2 * tb + tc
                                for nh in range(2):
                                    v.scalar_tensor_tensor(
                                        out=A(ch)[:, 512 * nh:512 * (nh + 1)],
                                        in0=accs[tc][nh][:, :], scalar=g_e[:, ch, e:e + 1],
                                        in1=A(ch)[:, 512 * nh:512 * (nh + 1)],
                                        op0=AO.mult, op1=AO.add)
                            sv.inc(v.last)
            V_(v5_all)
            assert sv.n == p5_sv0 + 9 * 128

            G_(g_fc)

            if dump("p5", [bufA[:]]):
                return nc

            p6_sv0, p6_ss0, p6_ld0, p6_sy0 = sv.n, ss.n, ld.n, sy.n
            V6 = lambda c, k: p6_sv0 + 3 * c + k
            S6 = lambda c, k: p6_ss0 + 2 * c + k
            L6 = lambda c: p6_ld0 + 16 * (c + 1)
            O6 = lambda c: p6_sy0 + 16 * (c + 1)

            def y_p6(y):
                for c in range(NCH):
                    if c == 0:
                        sv.wait(y, M["p5_ev_7_1_7"])
                        for cl in range(2):
                            ld.dma(y.dma_start(
                                out=x1_t[cl % 2],
                                in_=x1_d.rearrange("(c p) d -> p c d", p=128)[:, cl, :]))
                    if c + 2 < NCH:
                        sv.wait(y, V6(c, 3))
                        ld.dma(y.dma_start(
                            out=x1_t[c % 2],
                            in_=x1_d.rearrange("(c p) d -> p c d", p=128)[:, c + 2, :]))
                    else:
                        sv.wait(y, V6(c, 3))
                    sc = scr1k2 if c % 2 == 0 else scr1k
                    sy.dma(y.dma_start(
                        out=out_d.rearrange("(c p) d -> p c d", p=128)[:, c, :], in_=sc[:]))
            Y_(y_p6)
            assert sy.n == p6_sy0 + 16 * NCH

            def s_p6(s):
                for c in range(NCH):
                    if c == 0:
                        sv.wait(s, M["p5_ev_7_1_7"])
                    ss.inc(s.activation(out=a_nat[0], in_=A(c), func=AF.Square,
                                        accum_out=sumsq[:, c:c + 1]))
                    sv.wait(s, V6(c, 2))
                    pw.inc(s.activation(out=scr16[:, 1:2], in_=var_v[:, c:c + 1], func=AF.Ln))
                    pw.wait(s)
                    ss.inc(s.activation(out=r1_v[:, c:c + 1], in_=scr16[:, 1:2],
                                        func=AF.Exp, scale=-0.5))
            S_(s_p6)
            assert ss.n == p6_ss0 + 2 * NCH

            def v_p6(v):
                for c in range(NCH):
                    sv.inc(v.reduce_sum(out=sumv[:, c:c + 1], in_=A(c), axis=AX))
                    ss.wait(v, S6(c, 1))
                    sl = slice(c, c + 1)
                    v.tensor_scalar(out=mean_v[:, sl], in0=sumv[:, sl], scalar1=1.0 / D,
                                    scalar2=None, op0=AO.mult)
                    pv.inc(v.tensor_scalar(out=var_v[:, sl], in0=sumsq[:, sl],
                                           scalar1=1.0 / D, scalar2=None, op0=AO.mult))
                    pv.wait(v)
                    pv.inc(v.tensor_tensor(out=scr16[:, 0:1], in0=mean_v[:, sl],
                                           in1=mean_v[:, sl], op=AO.mult))
                    pv.wait(v)
                    pv.inc(v.tensor_tensor(out=var_v[:, sl], in0=var_v[:, sl],
                                           in1=scr16[:, 0:1], op=AO.subtract))
                    pv.wait(v)
                    sv.inc(v.tensor_scalar(out=var_v[:, sl], in0=var_v[:, sl], scalar1=EPS,
                                           scalar2=None, op0=AO.add))
                    ss.wait(v, S6(c, 2))
                    ld.wait(v, L6(c))
                    if c >= 2:
                        sy.wait(v, O6(c - 2))
                    sc = scr1k2 if c % 2 == 0 else scr1k
                    v.tensor_scalar(out=sc[:], in0=A(c), scalar1=mean_v[:, c:c + 1],
                                    scalar2=r1_v[:, c:c + 1], op0=AO.subtract, op1=AO.mult)
                    sv.inc(v.tensor_tensor(out=sc[:], in0=sc[:], in1=x1_t[c % 2],
                                           op=AO.add))
            V_(v_p6)
            assert sv.n == p6_sv0 + 3 * NCH

            def y_fin(y):
                sy.wait(y)
            Y_(y_fin)

    return nc


def make_consts():
    masks = np.zeros((128, 4 * 512), dtype=np.float32)
    for d in range(4):
        kk = np.arange(128)[:, None]
        qq = np.arange(512)[None, :]
        masks[:, 512 * d:512 * (d + 1)] = (128 * d + kk <= qq).astype(np.float32)
    kvec = np.tile(np.arange(1, 33, dtype=np.float32), 16)[None, :].repeat(128, 0)
    identb = np.eye(128, dtype=ml_dtypes.bfloat16)
    identf = np.eye(128, dtype=np.float32)
    return masks, np.ascontiguousarray(kvec), identb, identf


_PROG_CACHE = {}


def get_program(debug_phase=None):
    key = debug_phase
    if key not in _PROG_CACHE:
        _PROG_CACHE[key] = build_program(debug_phase)
    return _PROG_CACHE[key]


def make_in_maps(inputs):
    masks, kvec, identb, identf = make_consts()
    f32 = np.float32
    x = np.asarray(inputs["L0_attn_query"], f32)
    kp = np.ascontiguousarray(np.asarray(inputs["L0_attn_key_param"], f32)[0])
    base = dict(
        kp=kp,
        wqo=np.asarray(inputs["L0_attn_Wq"], f32),
        wko=np.asarray(inputs["L0_attn_Wk"], f32),
        wvo=np.asarray(inputs["L0_attn_Wv"], f32),
        woo=np.asarray(inputs["L0_attn_Wo"], f32),
        bq=np.ascontiguousarray(np.asarray(inputs["L0_attn_bq"], f32).reshape(8, 128)),
        bk=np.ascontiguousarray(np.asarray(inputs["L0_attn_bk"], f32).reshape(8, 128)),
        bv=np.ascontiguousarray(np.asarray(inputs["L0_attn_bv"], f32).reshape(1, D)),
        bo=np.ascontiguousarray(np.asarray(inputs["L0_attn_bo"], f32).reshape(1, D)),
        woi=np.asarray(inputs["L0_ffn_wo"], f32),
        rw=np.asarray(inputs["L0_ffn_router_w"], f32),
        fc1=np.ascontiguousarray(np.asarray(inputs["L0_ffn_exp_fc1"], f32).reshape(8 * D, 2048)),
        fc2=np.ascontiguousarray(np.asarray(inputs["L0_ffn_exp_fc2"], f32).reshape(8 * 2048, D)),
        masks=masks, kvec=kvec, ident_bf=identb, ident_f32=identf,
    )
    wq = np.asarray(inputs["L0_ffn_wq"], f32)
    wk = np.asarray(inputs["L0_ffn_wk"], f32)
    wv = np.asarray(inputs["L0_ffn_wv"], f32)
    in_maps = []
    for c in range(8):
        g, q = c // 4, c % 4
        m = dict(base)
        m["x0"] = np.ascontiguousarray(x[g, T_CORE * q:T_CORE * (q + 1)])
        m["wqi"] = np.ascontiguousarray(wq[:, 256 * q:256 * (q + 1)])
        m["wki"] = np.ascontiguousarray(wk[:, 256 * q:256 * (q + 1)])
        m["wvi"] = np.ascontiguousarray(wv[:, 256 * q:256 * (q + 1)])
        m["toffs"] = np.asarray([[2097152 * q + 128 * cc for cc in range(16)]],
                                dtype=np.int32)
        m["soffs"] = np.asarray([[2097152 * (cc % 4) + 64 * (4 * q + cc // 4)
                                  for cc in range(16)]], dtype=np.int32)
        in_maps.append(m)
    return in_maps


def kernel(**inputs):
    nc = get_program()
    in_maps = make_in_maps(inputs)
    res = run_bass_kernel_spmd(nc, in_maps, core_ids=list(range(8)))
    out = np.empty((2, T_FULL, D), dtype=np.float32)
    for c in range(8):
        g, q = c // 4, c % 4
        out[g, T_CORE * q:T_CORE * (q + 1)] = res.results[c]["out"]
    return out


# version 09
# speedup vs baseline: 2.2198x; 1.0003x over previous
import numpy as np
import ml_dtypes
from contextlib import ExitStack

import concourse.bass as bass
import concourse.mybir as mybir
from concourse.bass_types import AP
from concourse.bass_utils import run_bass_kernel_spmd

F32 = mybir.dt.float32
BF16 = mybir.dt.bfloat16
I32 = mybir.dt.int32
AO = mybir.AluOpType
AF = mybir.ActivationFunctionType
AX = mybir.AxisListType.X

D = 1024
T_CORE = 2048
T_FULL = 8192
NCH = 16
NTI = 4
EPS = 1e-5
RG = [[0, 1, 2, 3], [4, 5, 6, 7]]


def batcher_steps(n=32):
    steps = []
    p = 1
    while p < n:
        k = p
        while k >= 1:
            los = []
            j = k % p
            while j <= n - 1 - k:
                for i in range(0, min(k, n - j - k)):
                    lo = i + j
                    if (lo // (2 * p)) == ((lo + k) // (2 * p)):
                        los.append(lo)
                j += 2 * k
            steps.append((k, sorted(los)))
            k //= 2
        p *= 2
    return steps


def lo_pattern(los):
    los = list(los)
    r = 1
    while r < len(los) and los[r] == los[0] + r:
        r += 1
    runs = [los[i * r] for i in range(len(los) // r)]
    for i, st in enumerate(runs):
        assert los[i * r:(i + 1) * r] == list(range(st, st + r))
    if len(runs) == 1:
        return los[0], [(1, r)]
    s1 = runs[1] - runs[0]
    n1 = 1
    while n1 < len(runs) and runs[n1] == runs[0] + n1 * s1:
        n1 += 1
    if n1 == len(runs):
        return runs[0], [(s1, n1), (1, r)]
    s2 = runs[n1] - runs[0]
    blocks = [runs[i * n1] for i in range(len(runs) // n1)]
    for i, b in enumerate(blocks):
        assert runs[i * n1:(i + 1) * n1] == [b + jj * s1 for jj in range(n1)]
    return runs[0], [(s2, len(blocks)), (s1, n1), (1, r)]


SORT_STEPS = [(k, lo_pattern(los)) for k, los in batcher_steps(32)]


def carve(base, off, dims):
    pstep, pcount = base.ap[0]
    ap = [[pstep, pcount], [32, 16]] + [[s_, n_] for (s_, n_) in dims]
    return AP(base.tensor, base.offset + off, ap)



class _EngW:
    _SKIP = {"wait_ge", "nop", "engine_nop"}

    def __init__(self, eng):
        object.__setattr__(self, "_e", eng)
        object.__setattr__(self, "last", None)

    def __getattr__(self, n):
        a = getattr(self._e, n)
        if callable(a) and not n.startswith("_"):
            def call(*args, **kw):
                r = a(*args, **kw)
                if n not in _EngW._SKIP and r is not None and hasattr(r, "then_inc"):
                    object.__setattr__(self, "last", r)
                return r
            return call
        return a


class Ctr:
    __slots__ = ("sem", "n")

    def __init__(self, sem):
        self.sem = sem
        self.n = 0

    def inc(self, instr, k=1):
        instr.then_inc(self.sem, k)
        self.n += k
        return instr

    def dma(self, instr):
        instr.then_inc(self.sem, 16)
        self.n += 16
        return instr

    def wait(self, eng, n=None):
        eng.wait_ge(self.sem, self.n if n is None else n)


def build_program(debug_phase=None):
    nc = bass.Bass("TRN2", target_bir_lowering=False, debug=False)

    def din(name, shape, dt=F32):
        return nc.dram_tensor(name, shape, dt, kind="ExternalInput").ap()

    x0_d = din("x0", [T_CORE, D])
    kp_d = din("kp", [32, D])
    wqo_d = din("wqo", [D, D]); wko_d = din("wko", [D, D])
    wvo_d = din("wvo", [D, D]); woo_d = din("woo", [D, D])
    bq_d = din("bq", [8, 128]); bk_d = din("bk", [8, 128])
    bv_d = din("bv", [1, D]); bo_d = din("bo", [1, D])
    wqi_d = din("wqi", [D, 256]); wki_d = din("wki", [D, 256])
    wvi_d = din("wvi", [D, 256]); woi_d = din("woi", [D, D])
    rw_d = din("rw", [D, 8])
    fc1_d = din("fc1", [8 * D, 2048])
    fc2_d = din("fc2", [8 * 2048, D])
    mask_d = din("masks", [128, 4 * 512])
    kv_d = din("kvec", [128, 512])
    idb_d = din("ident_bf", [128, 128], BF16)
    idf_d = din("ident_f32", [128, 128])
    toff_d = din("toffs", [1, 16], I32)
    soff_d = din("soffs", [1, 16], I32)
    out_d = nc.dram_tensor("out", [T_CORE, D], F32, kind="ExternalOutput").ap()

    ago_in = nc.dram_tensor("ago_in", [T_CORE, D], BF16).ap()
    ago_out = nc.dram_tensor("ago_out", [T_FULL, D], BF16).ap()
    ag1_in = nc.dram_tensor("ag1_in", [4 * D, 512], BF16).ap()
    ag1_out = nc.dram_tensor("ag1_out", [16 * D, 512], BF16).ap()
    ag2_in = nc.dram_tensor("ag2_in", [1024, 2048], BF16).ap()
    ag2_out = nc.dram_tensor("ag2_out", [4096, 2048], BF16).ap()
    x1_d = nc.dram_tensor("x1_spill", [T_CORE, D], F32).ap()

    dbg_d = None
    if debug_phase is not None:
        dbg_d = nc.dram_tensor("dbg", [128, NCH * D], F32, kind="ExternalOutput").ap()

    ctx = ExitStack()

    def sbt(name, shape, dt):
        return ctx.enter_context(nc.sbuf_tensor(name, shape, dt))

    def pst(name, shape, dt=F32):
        return ctx.enter_context(nc.psum_tensor(name, shape, dt))

    bufA = sbt("bufA", [128, NCH * D], F32)

    def A(c):
        return bufA[:, c * D:(c + 1) * D]

    wpool = sbt("wpool", [128, 4 * 8 * D], BF16)
    WQO = wpool[:, 0:8 * D].rearrange("p (c n) -> p c n", c=8)
    WKO = wpool[:, 8 * D:16 * D].rearrange("p (c n) -> p c n", c=8)
    WVO = wpool[:, 16 * D:24 * D].rearrange("p (c n) -> p c n", c=8)
    WOO = wpool[:, 24 * D:32 * D].rearrange("p (c n) -> p c n", c=8)
    FC1H = [wpool[:, (2 * i) * 8 * D:(2 * i + 1) * 8 * D].rearrange("p (c n) -> p c n", c=8)
            for i in range(2)]
    FC2H = [wpool[:, (2 * i + 1) * 8 * D:(2 * i + 2) * 8 * D].rearrange("p (c n) -> p c n", c=8)
            for i in range(2)]

    QTI = wpool[:, 0:2 * T_FULL].rearrange("p (c n) -> p c n", c=2)
    KTI = wpool[:, 2 * T_FULL:4 * T_FULL].rearrange("p (c n) -> p c n", c=2)

    vpool = sbt("vpool", [128, 64 * 260], BF16)
    V_I = vpool[:].rearrange("p (c n) -> p c n", c=64)
    RT = vpool[:, :8 * T_CORE].rearrange("p (c n) -> p c n", c=8)
    ATP = vpool[:, :4096].rearrange("p (c n) -> p c n", c=8)

    wqi_s = sbt("wqi_s", [128, 8 * 256], BF16).rearrange("p (c n) -> p c n", c=8)
    wki_s = sbt("wki_s", [128, 8 * 256], BF16).rearrange("p (c n) -> p c n", c=8)
    rw_s = sbt("rw_s", [128, 8 * 8], BF16).rearrange("p (c n) -> p c n", c=8)
    woi_s = wpool[:, 2 * T_FULL:2 * T_FULL + 8 * D].rearrange("p (c n) -> p c n", c=8)

    kpT_s = vpool[:, 0:256]
    KPT = kpT_s[:].rearrange("p (c n) -> p c n", c=8)
    kT_o = vpool[:, 16384:16640]
    v_o = sbt("v_o", [64, D], BF16)
    bqc = sbt("bqc", [128, 8], F32)
    bkc = sbt("bkc", [128, 8], F32)
    bv_r = sbt("bv_r", [1, D], BF16)
    bo_r = sbt("bo_r", [1, D], BF16)

    ident = sbt("ident", [128, 128], BF16)
    identf = vpool[:, 256:512].bitcast(F32)
    ones_bf = sbt("ones_bf", [1, 128], BF16)
    toff_s = sbt("toff_s", [1, 16], I32)
    soff_s = sbt("soff_s", [1, 16], I32)

    qT_o = vpool[:, 4096:8192].rearrange("p (c n) -> p c n", c=8)
    z_both = sbt("z_both", [128, 1024], F32)
    z_sp = [z_both[:, 512 * i:512 * (i + 1)] for i in range(2)]
    masks = z_both[:].bitcast(BF16)
    tau = sbt("tau", [128, 16], F32)
    ksup = sbt("ksup", [128, 16], F32)
    csel = sbt("csel", [128, 16], F32)
    a_bf0 = sbt("a_bf0", [128, 512], BF16)
    a_bf = [a_bf0, a_bf0]
    at_lo = vpool[0:64, 12288:14336].rearrange("p (c n) -> p c n", c=4)
    at_hi = vpool[0:64, 14336:16384].rearrange("p (c n) -> p c n", c=4)
    aoT_s = vpool[:, 8192:12288].rearrange("p (c n) -> p c n", c=8)
    hT_t = [sbt(f"hT_{i}", [128, 8 * 512], BF16) for i in range(2)]
    HTT = [h[:].rearrange("p (c n) -> p c n", c=8) for h in hT_t]
    a_nat = [hT_t[0][:, 3072:4096], hT_t[1][:, 2048:3072]]

    sumv = sbt("sumv", [128, 16], F32)
    sumsq = sbt("sumsq", [128, 16], F32)
    mean_v = sbt("mean_v", [128, 16], F32)
    var_v = sbt("var_v", [128, 16], F32)
    r1_v = sbt("r1_v", [128, 16], F32)
    sd_v = sbt("sd_v", [128, 16], F32)
    m1_v = sbt("m1_v", [128, 16], F32)
    c2_v = sbt("c2_v", [128, 16], F32)
    scr1k = sbt("scr1k", [128, D], F32)
    scr1k2 = sbt("scr1k2", [128, D], F32)
    scr16 = sbt("scr16", [128, 16], F32)
    wvi_s = scr1k[:].bitcast(BF16).rearrange("p (c n) -> p c n", c=8)
    kp_s = scr1k2[0:32, 0:512].bitcast(BF16)
    bq_st = scr1k[0:8, 0:128]
    bk_st = scr1k[0:8, 128:256]
    den_s = scr1k2[0:1, 0:512]
    rec_s = scr1k2[0:1, 512:1024]
    recb = scr1k2[64:128, 0:512]
    ob_s = scr1k[:, 0:256].bitcast(BF16)
    lg8 = sbt("lg8", [128, 8], F32)
    pr8 = sbt("pr8", [128, 8], F32)
    pm8 = sbt("pm8", [128, 8], F32)
    msk8 = sbt("msk8", [128, 8], F32)
    msk8b = sbt("msk8b", [128, 8], F32)
    tv16 = sbt("tv16", [128, 16], F32)

    srt = hT_t[0][:].bitcast(F32)[:, 0:512]
    kvec = hT_t[0][:].bitcast(F32)[:, 1024:1536]
    ones_f = kvec[0:1, 0:64]
    srtmp = hT_t[0][:].bitcast(F32)[:, 512:1024]
    cum = hT_t[1][:].bitcast(F32)[:, 0:512]
    sup = hT_t[1][:].bitcast(F32)[:, 512:1024]
    e_bf = [hT_t[0][:, 512 * i:512 * (i + 1)] for i in range(4)]

    g_e = sbt("g_e", [128, 16 * 8], F32).rearrange("p (c n) -> p c n", c=16)
    h1sq = scr1k2[:, 512:768].bitcast(BF16)
    r_bf = scr1k2[:, 0:512].bitcast(BF16)
    x1_t = [hT_t[i][:].bitcast(F32)[:, 0:1024] for i in range(2)]

    ps_tr = pst("ps_tr", [128, 1024], BF16)
    ps_g1 = pst("ps_g1", [128, 512])
    ps_g2 = pst("ps_g2", [128, 512])
    ps_sA = [pst(f"ps_sA{i}", [128, 512]) for i in range(2)]
    ps_sB = [pst(f"ps_sB{i}", [128, 512]) for i in range(2)]
    ps_oA = pst("ps_oA", [128, 512])
    ps_oB = ps_g2
    PS_G = [ps_g1, ps_g2]

    names = ["ld", "wld", "sv", "ss", "st2", "cc", "sy", "pv", "pw", "fcl", "fc2"]
    C = {n: Ctr(ctx.enter_context(nc.semaphore(n))) for n in names}
    ld, wld, sv, ss, st2 = C["ld"], C["wld"], C["sv"], C["ss"], C["st2"]
    cc, sy, pv, fcl, pw, fc2 = C["cc"], C["sy"], C["pv"], C["fcl"], C["pw"], C["fc2"]

    M = {}

    stop = {"v": False}

    with ctx:
        with nc.Block() as block:
            def _wrap(reg):
                return lambda f: reg(lambda e: f(_EngW(e)))
            T_, V_, S_, G_, Y_ = (_wrap(block.tensor), _wrap(block.vector),
                                  _wrap(block.scalar), _wrap(block.gpsimd),
                                  _wrap(block.sync))

            def dump(tag, srcs):
                if debug_phase != tag:
                    return False

                def _y(y):
                    sv.wait(y)
                    ss.wait(y)
                    st2.wait(y)
                    off = 0
                    for src in srcs:
                        n = src.shape[-1]
                        sy.dma(y.dma_start(out=dbg_d[:src.shape[0], off:off + n], in_=src))
                        off += n
                    sy.wait(y)
                G_(_y)
                stop["v"] = True
                return True

            def y0(y):
                ld.dma(y.dma_start(out=bq_st, in_=bq_d[:, :]))
                ld.dma(y.dma_start(out=bk_st, in_=bk_d[:, :]))
                ld.dma(y.dma_start(out=kvec, in_=kv_d[:, :]))
                ld.dma(y.dma_start(out=ident[:], in_=idb_d[:, :]))
                ld.dma(y.dma_start(out=identf, in_=idf_d[:, :]))
                ld.dma(y.dma_start(out=toff_s[:], in_=toff_d[:, :]))
                ld.dma(y.dma_start(out=soff_s[:], in_=soff_d[:, :]))
            Y_(y0)
            M["p0_ldc"] = ld.n

            def y0x(y):
                for ti in range(NTI):
                    ld.dma(y.dma_start(
                        out=bufA[:, 4096 * ti:4096 * (ti + 1)].rearrange(
                            "p (c n) -> p c n", c=4),
                        in_=x0_d.rearrange("(t c p) d -> p t c d", p=128, t=NTI)[:, ti]))
                    M[f"p0_x0_{ti}"] = ld.n
            Y_(y0x)
            M["p0_ld"] = ld.n

            REGS = {}

            def y_regs(y):
                ld.wait(y, M["p0_ldc"])
            Y_(y_regs)
            toff_reg = ctx.enter_context(nc.scalar.register("toffr"))
            soff_reg = ctx.enter_context(nc.sync.register("soffr"))
            toff_reg2 = ctx.enter_context(nc.scalar.register("toffr2"))

            g0_wld0 = wld.n

            def g0(g):
                wld.dma(g.dma_start(out=kp_s, in_=kp_d[:, :]))
                wld.dma(g.dma_start(out=bv_r[:], in_=bv_d[:, :]))
                wld.dma(g.dma_start(out=bo_r[:], in_=bo_d[:, :]))
                for w_s, w_d in [(WKO, wko_d), (WVO, wvo_d), (WQO, wqo_d),
                                 (WOO, woo_d), (wqi_s, wqi_d),
                                 (wki_s, wki_d), (rw_s, rw_d)]:
                    wld.dma(g.dma_start(out=w_s, in_=w_d.rearrange("(c p) n -> p c n", p=128)))
            G_(g0)
            M["p0_kp"] = g0_wld0 + 3 * 16
            M["p0_wko"] = g0_wld0 + 4 * 16
            M["p0_wvo"] = g0_wld0 + 5 * 16
            M["p0_wqo"] = g0_wld0 + 6 * 16
            M["p0_woo"] = g0_wld0 + 7 * 16
            M["p0_wld"] = wld.n

            def v0(v):
                sv.inc(v.memset(ones_bf[:], 1.0))
            V_(v0)
            M["p0_sv"] = sv.n

            if dump("p00", [kvec, identf]):
                return nc

            def t0(t):
                ld.wait(t, M["p0_ldc"])
                wld.wait(t, M["p0_kp"])
                t.transpose(ps_g1[:, 0:8], in_=bq_st, identity=identf[0:8, 0:8])
                t.transpose(ps_g2[:, 0:8], in_=bk_st, identity=identf[0:8, 0:8])
                for j in range(8):
                    t.transpose(ps_tr[:, 32 * j:32 * j + 32],
                                in_=kp_s[:, 128 * j:128 * (j + 1)],
                                identity=ident[0:32, 0:32])
                st2.inc(t.last)
            T_(t0)
            M["p0_tr"] = st2.n

            if dump("p01", [kvec]):
                return nc

            def s0(s):
                st2.wait(s, M["p0_tr"])
                s.copy(out=bqc[:, :], in_=ps_g1[:, 0:8])
                s.copy(out=bkc[:, :], in_=ps_g2[:, 0:8])
                ss.inc(s.copy(out=kpT_s[:, :], in_=ps_tr[:, 0:256]))
            S_(s0)
            M["p0_ss"] = ss.n

            if dump("p02", [bqc[:], bkc[:]]):
                return nc

            def t1(t):
                ss.wait(t, M["p0_ss"])
                wld.wait(t, M["p0_wko"])
                for j in range(8):
                    if j >= 2:
                        sv.wait(t, M["p0_sv"] + (j - 1))
                    for dc in range(8):
                        t.matmul(ps_sA[j % 2][:, 0:32],
                                 lhsT=WKO[:, dc, 128 * j:128 * (j + 1)],
                                 rhs=KPT[:, dc, :].opt(), start=(dc == 0), stop=(dc == 7))
                    st2.inc(t.last)
            T_(t1)
            M["p0_kt"] = st2.n

            def v1(v):
                for j in range(8):
                    st2.wait(v, M["p0_tr"] + j + 1)
                    sv.inc(v.tensor_scalar(out=kT_o[:, 32 * j:32 * j + 32],
                                           in0=ps_sA[j % 2][:, 0:32],
                                           scalar1=bkc[:, j:j + 1], scalar2=None, op0=AO.add))
            V_(v1)
            M["p0_sv2"] = sv.n

            if dump("p03", [kT_o[:, :]]):
                return nc

            def t2(t):
                sv.wait(t, M["p0_sv2"])
                wld.wait(t, M["p0_wvo"])
                for nh in range(2):
                    for j in range(8):
                        t.matmul(ps_sB[nh][0:32, :], lhsT=kT_o[:, 32 * j:32 * j + 32],
                                 rhs=WVO[:, j, 512 * nh:512 * (nh + 1)],
                                 start=(j == 0), stop=False)
                    t.matmul(ps_sB[nh][0:32, :], lhsT=ones_bf[0:1, 0:32],
                             rhs=bv_r[0:1, 512 * nh:512 * (nh + 1)], start=False, stop=True)
                st2.inc(t.last)
            T_(t2)
            M["p0_vo"] = st2.n

            def s1(s):
                st2.wait(s, M["p0_vo"])
                for nh in range(2):
                    s.copy(out=v_o[0:32, 512 * nh:512 * (nh + 1)], in_=ps_sB[nh][0:32, :])
                    ss.inc(s.copy(out=v_o[32:64, 512 * nh:512 * (nh + 1)],
                                  in_=ps_sB[nh][0:32, :]))
            S_(s1)
            M["p0_end_ss"] = ss.n

            if dump("p0", [kT_o[:, :], v_o[:, :]]):
                return nc

            for ti in range(NTI):
                cs = [4 * ti + k for k in range(4)]

                def v_st(v, cs=cs, ti=ti):
                    ld.wait(v, M[f"p0_x0_{ti}"])
                    for c in cs:
                        v.reduce_sum(out=sumv[:, c:c + 1], in_=A(c), axis=AX)
                    sv.inc(v.last)
                V_(v_st)

                def s_st(s, cs=cs, ti=ti):
                    ld.wait(s, M[f"p0_x0_{ti}"])
                    if ti == 0:
                        st2.wait(s, M["p0_tr"])
                    for c in cs:
                        s.activation(out=scr1k[:], in_=A(c), func=AF.Square,
                                     accum_out=sumsq[:, c:c + 1])
                    ss.inc(s.last)
                S_(s_st)
                M[f"p1_stats_{ti}"] = (sv.n, ss.n)

                def v_mz(v, cs=cs, ti=ti):
                    ss.wait(v, M[f"p1_stats_{ti}"][1])
                    sv.wait(v, M[f"p1_stats_{ti}"][0])
                    sl = slice(cs[0], cs[0] + 4)
                    v.tensor_scalar(out=mean_v[:, sl], in0=sumv[:, sl], scalar1=1.0 / D,
                                    scalar2=None, op0=AO.mult)
                    pv.inc(v.tensor_scalar(out=var_v[:, sl], in0=sumsq[:, sl],
                                           scalar1=1.0 / D, scalar2=None, op0=AO.mult))
                    pv.wait(v)
                    pv.inc(v.tensor_tensor(out=scr16[:, sl], in0=mean_v[:, sl],
                                           in1=mean_v[:, sl], op=AO.mult))
                    pv.wait(v)
                    pv.inc(v.tensor_tensor(out=var_v[:, sl], in0=var_v[:, sl],
                                           in1=scr16[:, sl], op=AO.subtract))
                    pv.wait(v)
                    sv.inc(v.tensor_scalar(out=var_v[:, sl], in0=var_v[:, sl], scalar1=EPS,
                                           scalar2=None, op0=AO.add))
                V_(v_mz)
                M[f"p1_var_{ti}"] = sv.n

                def s_rs(s, cs=cs, ti=ti):
                    sv.wait(s, M[f"p1_var_{ti}"])
                    sl = slice(cs[0], cs[0] + 4)
                    pw.inc(s.activation(out=scr16[:, sl], in_=var_v[:, sl], func=AF.Ln))
                    pw.wait(s)
                    ss.inc(s.activation(out=r1_v[:, sl], in_=scr16[:, sl], func=AF.Exp,
                                        scale=-0.5))
                S_(s_rs)
                M[f"p1_rstd_{ti}"] = ss.n

                for ci, c in enumerate(cs):
                    def v_ap(v, c=c, ci=ci, ti=ti):
                        if ci == 0:
                            ss.wait(v, M[f"p1_rstd_{ti}"])
                        key = f"p1_aTtr_{ti}_{ci - 2}" if ci >= 2 else (
                            f"p1_aTtr_{ti - 1}_{ci + 2}" if ti > 0 else None)
                        if key is not None:
                            st2.wait(v, M[key])
                        sv.inc(v.tensor_scalar(out=a_nat[ci % 2], in0=A(c),
                                               scalar1=mean_v[:, c:c + 1],
                                               scalar2=r1_v[:, c:c + 1],
                                               op0=AO.subtract, op1=AO.mult))
                    V_(v_ap)
                    M[f"p1_anat_{ti}_{ci}"] = sv.n

                    def t_tr(t, ti=ti, ci=ci):
                        sv.wait(t, M[f"p1_anat_{ti}_{ci}"])
                        kprev = (f"p1_aTev_{ti}_{ci - 1}" if ci > 0 else
                                 (f"p1_aev_{ti - 1}_3" if ti > 0 else None))
                        if kprev is not None:
                            ss.wait(t, M[kprev])
                        for j in range(8):
                            t.transpose(ps_tr[:, 128 * j:128 * (j + 1)],
                                        in_=a_nat[ci % 2][:, 128 * j:128 * (j + 1)],
                                        identity=ident[:])
                        st2.inc(t.last)
                    T_(t_tr)
                    M[f"p1_aTtr_{ti}_{ci}"] = st2.n

                    def s_ev(s, ti=ti, ci=ci):
                        st2.wait(s, M[f"p1_aTtr_{ti}_{ci}"])
                        ss.inc(s.copy(out=ATP[:, :, 128 * ci:128 * (ci + 1)],
                                      in_=ps_tr[:].rearrange("p (c n) -> p c n", c=8)))
                    S_(s_ev)
                    M[f"p1_aTev_{ti}_{ci}"] = ss.n
                    if ti == 0 and ci == 0 and dump("d_ln1", [a_nat[0]]):
                        return nc

                qev_start = sv.n

                def t_q(t, ti=ti, qev_start=qev_start):
                    ss.wait(t, M[f"p1_aTev_{ti}_3"])
                    if ti == 0:
                        wld.wait(t, M["p0_wqo"])
                    for j in range(8):
                        if j >= 2:
                            sv.wait(t, qev_start + (j - 1))
                        elif ti > 0:
                            ss.wait(t, M[f"p1_oev_{ti - 1}_3"])
                        for dc in range(8):
                            t.matmul(PS_G[j % 2][:, :],
                                     lhsT=WQO[:, dc, 128 * j:128 * (j + 1)],
                                     rhs=ATP[:, dc, :].opt(), start=(dc == 0), stop=(dc == 7))
                        st2.inc(t.last)
                T_(t_q)
                M[f"p1_qmm_{ti}"] = st2.n
                M[f"p1_qev_{ti}"] = qev_start + 8

                def v_qe(v, ti=ti):
                    for j in range(8):
                        st2.wait(v, M[f"p1_qmm_{ti}"] - 8 + j + 1)
                        sv.inc(v.tensor_scalar(out=qT_o[:, j, :].opt(), in0=PS_G[j % 2][:, :],
                                               scalar1=bqc[:, j:j + 1], scalar2=None,
                                               op0=AO.add))
                V_(v_qe)
                assert sv.n == M[f"p1_qev_{ti}"]
                if ti == 0 and dump("d_q", [vpool[:, 4096:8192]]):
                    return nc

                for ci, c in enumerate(cs):
                    def t_sc(t, ti=ti, ci=ci):
                        if ci == 0:
                            sv.wait(t, M[f"p1_qev_{ti}"])
                        kprev = (f"p1_zev_{ti}_{ci - 2}" if ci >= 2 else
                                 (f"p1_zev_{ti - 1}_{ci + 2}" if ti > 0 else None))
                        if kprev is not None:
                            ss.wait(t, M[kprev])
                        phA, phB = ps_sA[ci % 2], ps_sB[ci % 2]
                        for pj in range(8):
                            t.matmul(phA[:, 32 * pj:32 * pj + 32],
                                     lhsT=qT_o[0:64, pj, 128 * ci:128 * (ci + 1)].opt(),
                                     rhs=kT_o[0:64, 32 * pj:32 * pj + 32],
                                     start=True, stop=True)
                            t.matmul(phB[:, 32 * pj:32 * pj + 32],
                                     lhsT=qT_o[64:128, pj, 128 * ci:128 * (ci + 1)].opt(),
                                     rhs=kT_o[64:128, 32 * pj:32 * pj + 32],
                                     start=True, stop=True, tile_position=(64, 0))
                        st2.inc(t.last)
                    T_(t_sc)
                    M[f"p1_smm_{ti}_{ci}"] = st2.n

                    def s_z(s, ti=ti, ci=ci):
                        st2.wait(s, M[f"p1_smm_{ti}_{ci}"])
                        kprev = (f"p1_spx_{ti}_{ci - 2}" if ci >= 2 else
                                 (f"p1_spx_{ti - 1}_{ci + 2}" if ti > 0 else None))
                        if kprev is not None:
                            sv.wait(s, M[kprev])
                        zv = z_sp[ci % 2].rearrange("p (a b c) -> p a b c", a=8, b=2)
                        s.mul(out=zv[:, :, 0, :],
                              in_=ps_sA[ci % 2][:, 0:256].rearrange("p (a c) -> p a c", a=8),
                              mul=0.125)
                        ss.inc(s.mul(out=zv[:, :, 1, :],
                                     in_=ps_sB[ci % 2][:, 0:256].rearrange("p (a c) -> p a c", a=8),
                                     mul=0.125))
                    S_(s_z)
                    M[f"p1_zev_{ti}_{ci}"] = ss.n
                    if ti == 0 and ci == 0 and dump("d_z", [z_sp[0]]):
                        return nc

                    def v_sp(v, ti=ti, ci=ci):
                        ss.wait(v, M[f"p1_zev_{ti}_{ci}"])
                        kprev = (f"p1_atr_{ti}_{ci - 2}" if ci >= 2 else
                                 (f"p1_atr_{ti - 1}_{ci + 2}" if ti > 0 else None))
                        if kprev is not None:
                            st2.wait(v, M[kprev])
                        z = z_sp[ci % 2]
                        v.tensor_copy(out=srt, in_=z[:])
                        for k, (off, dims) in SORT_STEPS:
                            lo = carve(srt, off, dims)
                            hi = carve(srt, off + k, dims)
                            tmp = carve(srtmp, off, dims)
                            v.tensor_tensor(out=tmp, in0=lo, in1=hi, op=AO.min)
                            v.tensor_tensor(out=lo, in0=lo, in1=hi, op=AO.max)
                            v.tensor_copy(out=hi, in_=tmp)
                        for h in range(16):
                            v.tensor_tensor_scan(out=cum[:, 32 * h:32 * h + 32],
                                                 data0=srt[:, 32 * h:32 * h + 32],
                                                 data1=srt[:, 32 * h:32 * h + 32],
                                                 initial=0.0, op0=AO.add, op1=AO.bypass)
                        pv.inc(v.last)
                        pv.wait(v)
                        v.tensor_tensor(out=sup, in0=kvec, in1=srt, op=AO.mult)
                        v.tensor_scalar(out=sup, in0=sup, scalar1=1.0, scalar2=None,
                                        op0=AO.add)
                        v.tensor_tensor(out=sup, in0=sup, in1=cum, op=AO.is_gt)
                        v.reduce_sum(out=ksup[:, :],
                                     in_=sup.rearrange("p (h s) -> p h s", h=16), axis=AX)
                        v.tensor_tensor(out=srtmp, in0=sup, in1=srt, op=AO.mult)
                        pv.inc(v.reduce_sum(out=csel[:, :],
                                            in_=srtmp.rearrange("p (h s) -> p h s", h=16),
                                            axis=AX))
                        pv.wait(v)
                        v.tensor_scalar(out=csel[:, :], in0=csel[:, :], scalar1=1.0,
                                        scalar2=None, op0=AO.subtract)
                        pv.inc(v.reciprocal(out=ksup[:, :], in_=ksup[:, :]))
                        pv.wait(v)
                        pv.inc(v.tensor_tensor(out=tau[:, :], in0=csel[:, :], in1=ksup[:, :],
                                               op=AO.mult))
                        pv.wait(v)
                        for h in range(16):
                            v.tensor_scalar(out=a_bf[ci % 2][:, 32 * h:32 * h + 32],
                                            in0=z[:, 32 * h:32 * h + 32],
                                            scalar1=tau[:, h:h + 1], scalar2=0.0,
                                            op0=AO.subtract, op1=AO.max)
                        sv.inc(v.last)
                    V_(v_sp)
                    M[f"p1_spx_{ti}_{ci}"] = sv.n
                    if ti == 0 and ci == 0 and dump("d_A", [a_bf[0][:]]):
                        return nc

                    def t_at(t, ti=ti, ci=ci):
                        sv.wait(t, M[f"p1_spx_{ti}_{ci}"])
                        ss.wait(t, M[f"p1_aev_{ti}_{ci - 1}"] if ci > 0
                                else M[f"p1_aTev_{ti}_3"])
                        for kpair in range(8):
                            gg, hi = kpair % 4, kpair // 4
                            col = 128 * gg + 512 * hi
                            h0 = 4 * gg + 2 * hi
                            t.transpose(ps_tr[0:64, col:col + 128],
                                        in_=a_bf[ci % 2][:, 32 * h0:32 * h0 + 64],
                                        identity=ident[:])
                        st2.inc(t.last)
                    T_(t_at)
                    M[f"p1_atr_{ti}_{ci}"] = st2.n

                    def s_ae(s, ti=ti, ci=ci):
                        st2.wait(s, M[f"p1_atr_{ti}_{ci}"])
                        s.copy(out=at_lo[:, :, 128 * ci:128 * (ci + 1)],
                               in_=ps_tr[0:64, 0:512].rearrange("p (c n) -> p c n", c=4))
                        s.copy(out=at_hi[:, :, 128 * ci:128 * (ci + 1)],
                               in_=ps_tr[0:64, 512:1024].rearrange("p (c n) -> p c n", c=4))
                        ss.inc(s.last)
                    S_(s_ae)
                    M[f"p1_aev_{ti}_{ci}"] = ss.n

                for ci, c in enumerate(cs):
                    def t_avn(t, ti=ti, ci=ci):
                        ss.wait(t, M[f"p1_aev_{ti}_{ci}"])
                        kprev = (f"p1_oev_{ti}_{ci - 1}" if ci > 0 else
                                 (f"p1_oev_{ti - 1}_3" if ti > 0 else None))
                        if kprev is not None:
                            ss.wait(t, M[kprev])
                        for h in range(16):
                            m = h % 4
                            srcT = at_lo if m < 2 else at_hi
                            rr = 32 * (m % 2)
                            ph = PS_G[h % 2]
                            t.matmul(ph[:, 64 * (h // 2):64 * (h // 2) + 64],
                                     lhsT=srcT[rr:rr + 32, h // 4,
                                               128 * ci:128 * (ci + 1)].opt(),
                                     rhs=v_o[rr:rr + 32, 64 * h:64 * h + 64],
                                     start=True, stop=True)
                        st2.inc(t.last)
                    T_(t_avn)
                    M[f"p1_avn_{ti}_{ci}"] = st2.n

                    def s_oe(s, ti=ti, ci=ci, c=c):
                        st2.wait(s, M[f"p1_avn_{ti}_{ci}"])
                        if ti + ci > 0:
                            ld.wait(s, M.get(f"p1_ogw_{ti}_{ci - 1}",
                                             M.get(f"p1_ogw_{ti - 1}_3", 0)))
                        sl = vpool[:, 8192 + 1024 * (c % 2):8192 + 1024 * (c % 2) + 1024]
                        slv = sl.rearrange("p (a b c) -> p a b c", a=8, b=2)
                        s.copy(out=slv[:, :, 0, :],
                               in_=ps_g1[:].rearrange("p (a c) -> p a c", a=8))
                        s.copy(out=slv[:, :, 1, :],
                               in_=ps_g2[:].rearrange("p (a c) -> p a c", a=8))
                        ss.inc(s.last)
                    S_(s_oe)
                    M[f"p1_oev_{ti}_{ci}"] = ss.n

                    def y_og(y, ti=ti, ci=ci, c=c):
                        ss.wait(y, M[f"p1_oev_{ti}_{ci}"])
                        sl = vpool[:, 8192 + 1024 * (c % 2):8192 + 1024 * (c % 2) + 1024]
                        ld.dma(y.dma_start(
                            out=ago_in.rearrange("(c p) d -> p c d", p=128)[:, c, :],
                            in_=sl))
                    Y_(y_og)
                    M[f"p1_ogw_{ti}_{ci}"] = ld.n

            def g_ago(g):
                ld.wait(g, M["p1_ogw_3_3"])
                cc.inc(g.collective_compute("AllGather", AO.bypass, replica_groups=RG,
                                            ins=[ago_in.opt()], outs=[ago_out.opt()]))
            G_(g_ago)
            M["ago"] = cc.n

            ago_h = ago_out.tensor
    
            p1b_ld0, p1b_st0, p1b_ss0, p1b_sv0 = ld.n, st2.n, ss.n, sv.n
            T1b = lambda c, k: p1b_st0 + 2 * c + k
            S1b = lambda c, k: p1b_ss0 + 3 * c + k
            V1b = lambda c, k: p1b_sv0 + 2 * c + k

            def y_1b(y):
                for c in range(NCH):
                    if c == 0:
                        cc.wait(y, M["ago"])
                    if c >= 2:
                        st2.wait(y, T1b(c - 2, 1))
                    y.reg_load(soff_reg, soff_s[0:1, c:c + 1])
                    in_ap = AP(ago_h, soff_reg, [[16384, 128], [1024, 16], [1, 64]])
                    ld.dma(y.dma_start(
                        out=hT_t[c % 2][:, 0:1024].rearrange("p (a b) -> p a b", a=16),
                        in_=in_ap))
            Y_(y_1b)
            assert ld.n == p1b_ld0 + 16 * NCH

            def t_1b(t):
                for c in range(NCH):
                    ld.wait(t, p1b_ld0 + 16 * (c + 1))
                    if c == 0:
                        ss.wait(t, M["p1_oev_3_3"])
                    else:
                        ss.wait(t, S1b(c - 1, 1))
                    for j in range(8):
                        t.transpose(ps_tr[:, 128 * j:128 * (j + 1)],
                                    in_=hT_t[c % 2][:, 128 * j:128 * (j + 1)],
                                    identity=ident[:])
                    st2.inc(t.last)
                    ss.wait(t, S1b(c, 1))
                    if c == 0:
                        wld.wait(t, M["p0_woo"])
                    if c > 0:
                        sv.wait(t, V1b(c - 1, 2))
                    for nh in range(2):
                        ph = PS_G[nh]
                        for j in range(8):
                            t.matmul(ph[:, :],
                                     lhsT=ATP[:, j, 128 * (c % 2):128 * (c % 2) + 128].opt(),
                                     rhs=WOO[:, j, 512 * nh:512 * (nh + 1)],
                                     start=(j == 0), stop=False)
                        t.matmul(ph[:, :], lhsT=ones_bf[0:1, 0:128],
                                 rhs=bo_r[0:1, 512 * nh:512 * (nh + 1)],
                                 start=False, stop=True)
                    st2.inc(t.last)
            T_(t_1b)
            assert st2.n == p1b_st0 + 2 * NCH

            def s_1b(s):
                for c in range(NCH):
                    st2.wait(s, T1b(c, 1))
                    if c >= 2:
                        st2.wait(s, T1b(c - 2, 2))
                    ss.inc(s.copy(out=ATP[:, :, 128 * (c % 2):128 * (c % 2) + 128],
                                  in_=ps_tr[:].rearrange("p (c n) -> p c n", c=8)))
                    st2.wait(s, T1b(c, 2))
                    s.activation(out=scr1k[:, 0:512], in_=ps_g1[:], func=AF.Square,
                                 accum_out=scr16[:, 0:1])
                    ss.inc(s.activation(out=scr1k[:, 0:512], in_=ps_g2[:], func=AF.Square,
                                        accum_out=scr16[:, 1:2]))
                    sv.wait(s, V1b(c, 1))
                    pw.inc(s.activation(out=scr16[:, 7:8], in_=scr16[:, 5:6], func=AF.Ln))
                    pw.wait(s)
                    ss.inc(s.activation(out=scr16[:, 7:8], in_=scr16[:, 7:8],
                                        func=AF.Exp, scale=-0.5))
            S_(s_1b)
            assert ss.n == p1b_ss0 + 3 * NCH

            def v_1b(v):
                for c in range(NCH):
                    ss.wait(v, S1b(c, 2))
                    v.reduce_sum(out=scr16[:, 2:3], in_=ps_g1[:], axis=AX)
                    pv.inc(v.reduce_sum(out=scr16[:, 3:4], in_=ps_g2[:], axis=AX))
                    pv.wait(v)
                    pv.inc(v.tensor_tensor(out=scr16[:, 4:5], in0=scr16[:, 2:3],
                                           in1=scr16[:, 3:4], op=AO.add))
                    pv.wait(v)
                    pv.inc(v.tensor_tensor(out=scr16[:, 5:6], in0=scr16[:, 0:1],
                                           in1=scr16[:, 1:2], op=AO.add))
                    pv.wait(v)
                    v.tensor_scalar(out=scr16[:, 4:5], in0=scr16[:, 4:5],
                                    scalar1=1.0 / D, scalar2=None, op0=AO.mult)
                    pv.inc(v.tensor_scalar(out=scr16[:, 5:6], in0=scr16[:, 5:6],
                                           scalar1=1.0 / D, scalar2=None, op0=AO.mult))
                    pv.wait(v)
                    pv.inc(v.tensor_tensor(out=scr16[:, 6:7], in0=scr16[:, 4:5],
                                           in1=scr16[:, 4:5], op=AO.mult))
                    pv.wait(v)
                    pv.inc(v.tensor_tensor(out=scr16[:, 5:6], in0=scr16[:, 5:6],
                                           in1=scr16[:, 6:7], op=AO.subtract))
                    pv.wait(v)
                    sv.inc(v.tensor_scalar(out=scr16[:, 5:6], in0=scr16[:, 5:6],
                                           scalar1=EPS, scalar2=None, op0=AO.add))
                    ss.wait(v, S1b(c, 3))
                    pv.inc(v.engine_nop())
                    pv.wait(v)
                    for nh in range(2):
                        v.tensor_scalar(out=scr1k2[:, 512 * nh:512 * (nh + 1)],
                                        in0=PS_G[nh][:, :], scalar1=scr16[:, 4:5],
                                        scalar2=scr16[:, 7:8],
                                        op0=AO.subtract, op1=AO.mult)
                    sv.inc(v.tensor_tensor(out=A(c), in0=A(c), in1=scr1k2[:],
                                           op=AO.add))
            V_(v_1b)
            assert sv.n == p1b_sv0 + 2 * NCH
            for c in range(NCH):
                M[f"p1_x1_{c}"] = V1b(c, 2)

            if dump("d_x1c", [A(0), scr1k2[:], scr16[:, 0:8]]):
                return nc

            if dump("p1", [bufA[:]]):
                return nc

            p15_sv0 = sv.n
            p15_ss0 = ss.n
            p15_st0 = st2.n
            p15_ld0 = ld.n
            VS = lambda c, k: p15_sv0 + 4 * c + k
            SS = lambda c, k: p15_ss0 + 4 * c + k
            TS_ = lambda c: p15_st0 + c + 1
            LD = lambda c, k: p15_ld0 + 32 * c + 16 * k

            def y_15(y):
                for c in range(NCH):
                    sv.wait(y, M[f"p1_x1_{c}"])
                    ld.dma(y.dma_start(
                        out=x1_d.rearrange("(c p) d -> p c d", p=128)[:, c, :], in_=A(c)))
                    ss.wait(y, SS(c, 4))
                    ld.dma(y.dma_start(
                        out=ag1_in.rearrange("(k j p) t -> p k j t", p=128, k=4)
                        [:, c // 4, :, 128 * (c % 4):128 * (c % 4) + 128],
                        in_=hT_t[c % 2][:, 0:1024].rearrange("p (j n) -> p j n", j=8)))
            Y_(y_15)
            assert ld.n == p15_ld0 + 32 * NCH
            for c in range(NCH):
                M[f"p15_hw_{c}"] = LD(c, 2)
            M["p15_hw_15"] = ld.n

            def v_15(v):
                for c in range(NCH):
                    sv.inc(v.reduce_sum(out=sumv[:, c:c + 1], in_=A(c), axis=AX))
                    ss.wait(v, SS(c, 1))
                    sv.wait(v, VS(c, 1))
                    sl = slice(c, c + 1)
                    v.tensor_scalar(out=m1_v[:, sl], in0=sumv[:, sl], scalar1=1.0 / D,
                                    scalar2=None, op0=AO.mult)
                    pv.inc(v.tensor_scalar(out=var_v[:, sl], in0=sumsq[:, sl],
                                           scalar1=1.0 / D, scalar2=None, op0=AO.mult))
                    pv.wait(v)
                    pv.inc(v.tensor_tensor(out=scr16[:, 0:1], in0=m1_v[:, sl],
                                           in1=m1_v[:, sl], op=AO.mult))
                    pv.wait(v)
                    pv.inc(v.tensor_tensor(out=var_v[:, sl], in0=var_v[:, sl],
                                           in1=scr16[:, 0:1], op=AO.subtract))
                    pv.wait(v)
                    sv.inc(v.tensor_scalar(out=var_v[:, sl], in0=var_v[:, sl], scalar1=EPS,
                                           scalar2=None, op0=AO.add))
                    ss.wait(v, SS(c, 2))
                    v.tensor_tensor(out=scr16[:, 2:3], in0=r1_v[:, sl], in1=r1_v[:, sl],
                                    op=AO.mult)
                    pv.inc(v.tensor_scalar(out=scr16[:, 3:4], in0=var_v[:, sl], scalar1=EPS,
                                           scalar2=None, op0=AO.subtract))
                    pv.wait(v)
                    pv.inc(v.tensor_tensor(out=scr16[:, 2:3], in0=scr16[:, 2:3],
                                           in1=scr16[:, 3:4], op=AO.mult))
                    pv.wait(v)
                    sv.inc(v.tensor_scalar(out=scr16[:, 2:3], in0=scr16[:, 2:3], scalar1=EPS,
                                           scalar2=None, op0=AO.add))
                    ss.wait(v, SS(c, 3))
                    ld.wait(v, LD(c, 1))
                    if c >= 2:
                        ld.wait(v, LD(c - 2, 2))
                        st2.wait(v, TS_(c - 2))
                    pv.inc(v.tensor_tensor(out=scr16[:, 4:5], in0=r1_v[:, c:c + 1],
                                           in1=c2_v[:, c:c + 1], op=AO.mult))
                    pv.wait(v)
                    sv.inc(v.tensor_scalar(out=a_nat[c % 2], in0=A(c),
                                           scalar1=m1_v[:, c:c + 1],
                                           scalar2=scr16[:, 4:5],
                                           op0=AO.subtract, op1=AO.mult))
                    v.tensor_scalar(out=A(c), in0=A(c), scalar1=m1_v[:, c:c + 1],
                                    scalar2=r1_v[:, c:c + 1], op0=AO.subtract, op1=AO.mult)
            V_(v_15)
            assert sv.n == p15_sv0 + 4 * NCH
            M["p15_vend"] = sv.n

            def s_15(s):
                for c in range(NCH):
                    if c == 0:
                        sv.wait(s, M["p1_x1_15"])
                    ss.inc(s.activation(out=scr1k[:], in_=A(c), func=AF.Square,
                                        accum_out=sumsq[:, c:c + 1]))
                    sv.wait(s, VS(c, 2))
                    sl = slice(c, c + 1)
                    pw.inc(s.activation(out=scr16[:, 1:2], in_=var_v[:, sl], func=AF.Ln))
                    pw.wait(s)
                    s.activation(out=r1_v[:, sl], in_=scr16[:, 1:2], func=AF.Exp, scale=-0.5)
                    ss.inc(s.activation(out=sd_v[:, sl], in_=scr16[:, 1:2], func=AF.Exp,
                                        scale=0.5))
                    sv.wait(s, VS(c, 3))
                    pw.inc(s.activation(out=scr16[:, 5:6], in_=scr16[:, 2:3], func=AF.Ln))
                    pw.wait(s)
                    ss.inc(s.activation(out=c2_v[:, c:c + 1], in_=scr16[:, 5:6], func=AF.Exp,
                                        scale=-0.5))
                    st2.wait(s, TS_(c))
                    if c >= 2:
                        ld.wait(s, LD(c - 2, 2))
                    ss.inc(s.copy(out=hT_t[c % 2][:, 0:1024], in_=ps_tr[:]))
            S_(s_15)
            assert ss.n == p15_ss0 + 4 * NCH
            M["p15_send"] = ss.n

            def t_15(t):
                for c in range(NCH):
                    sv.wait(t, VS(c, 4))
                    if c == 0:
                        ss.wait(t, M["p1_oev_3_3"])
                    else:
                        ss.wait(t, SS(c - 1, 4))
                    for j in range(8):
                        t.transpose(ps_tr[:, 128 * j:128 * (j + 1)],
                                    in_=a_nat[c % 2][:, 128 * j:128 * (j + 1)],
                                    identity=ident[:])
                    st2.inc(t.last)
            T_(t_15)
            assert st2.n == p15_st0 + NCH

            if dump("p15", [bufA[:]]):
                return nc

            def g_mask(g):
                sv.wait(g, M["p1_spx_3_3"])
                wld.dma(g.dma_start(out=masks, in_=mask_d[:, :]))
                ss.wait(g, M["p15_send"])
                sv.wait(g, M["p15_vend"])
                wld.dma(g.dma_start(out=wvi_s,
                                    in_=wvi_d.rearrange("(c p) n -> p c n", p=128)))
            G_(g_mask)
            M["wvi_ld"] = wld.n

            ag1_c0 = cc.n

            def g_a1(g):
                for ks in range(4):
                    ld.wait(g, M[f"p15_hw_{4 * ks + 3}"])
                    cc.inc(g.collective_compute(
                        "AllGather", AO.bypass, replica_groups=RG,
                        ins=[ag1_in[1024 * ks:1024 * (ks + 1), :].opt()],
                        outs=[ag1_out[4096 * ks:4096 * (ks + 1), :].opt()]))
            G_(g_a1)
            M["ag1"] = cc.n

            ORD3A = [4 * r + k for k in range(4) for r in range(4)]
            for oi, tt in enumerate(ORD3A):
                def y_ht(y, oi=oi, tt=tt):
                    k, r = tt % 4, tt // 4
                    if oi % 4 == 0:
                        cc.wait(y, ag1_c0 + k + 1)
                    if oi >= 2:
                        st2.wait(y, M[f"p3a_v_{ORD3A[oi - 2]}"])
                    ld.dma(y.dma_start(
                        out=HTT[oi % 2],
                        in_=ag1_out.rearrange("(k r j p) t -> p k r j t", k=4, r=4, j=8)
                        [:, k, r, :, :]))
                Y_(y_ht)
                M[f"p3a_ld_{tt}"] = ld.n

                def t_qkv(t, oi=oi, tt=tt):
                    ld.wait(t, M[f"p3a_ld_{tt}"])
                    if oi == 0:
                        wld.wait(t, M["wvi_ld"])
                    ht = HTT[oi % 2]
                    if oi > 0:
                        sv.wait(t, M[f"p3a_qke_{ORD3A[oi - 1]}"])
                    for wi, w in enumerate([wqi_s, wki_s]):
                        phs = PS_G if wi == 0 else ps_sB
                        for cj in range(2):
                            for dc in range(8):
                                t.matmul(phs[cj][:, :], lhsT=w[:, dc, 128 * cj:128 * (cj + 1)],
                                         rhs=ht[:, dc, :].opt(), start=(dc == 0), stop=(dc == 7))
                        st2.inc(t.last)
                    ve_base = ss.n
                    for tc in range(4):
                        if tc >= 2:
                            ss.wait(t, ve_base + (tc - 1))
                        for dc in range(8):
                            t.matmul(ps_sA[tc % 2][:, 0:256],
                                     lhsT=ht[:, dc, 128 * tc:128 * (tc + 1)].opt(),
                                     rhs=wvi_s[:, dc, :], start=(dc == 0), stop=(dc == 7))
                        st2.inc(t.last)
                T_(t_qkv)
                M[f"p3a_t_{tt}"] = st2.n
                M[f"p3a_v_{tt}"] = st2.n

                def v_qk(v, oi=oi, tt=tt):
                    if oi == 0:
                        vi = vpool[:].rearrange("p (c m n) -> p c m n", c=64, m=4)
                        v.memset(vi[:, :, :, 64:65], 1.0)
                    base = M[f"p3a_t_{tt}"] - 6
                    for wi, dst in [(0, QTI), (1, KTI)]:
                        st2.wait(v, base + wi + 1)
                        phs = PS_G if wi == 0 else ps_sB
                        for cj in range(2):
                            v.tensor_copy(out=dst[:, cj, 512 * tt:512 * (tt + 1)].opt(),
                                          in_=phs[cj][:, :])
                        sv.inc(v.last)
                V_(v_qk)
                M[f"p3a_qke_{tt}"] = sv.n

                def s_v(s, tt=tt):
                    base = M[f"p3a_t_{tt}"] - 4
                    for tc in range(4):
                        st2.wait(s, base + tc + 1)
                        kc = 4 * tt + tc
                        for m in range(4):
                            s.copy(out=V_I[:, kc, 65 * m:65 * m + 64].opt(),
                                   in_=ps_sA[tc % 2][:, 64 * m:64 * m + 64])
                        ss.inc(s.last)
                S_(s_v)
                M[f"p3a_ve_{tt}"] = ss.n
                if oi == 0 and dump("d_h", [hT_t[0][:, 0:4096], ag1_out[0:128, :]]):
                    return nc

            if dump("p3q", [wpool[:, 0:4096], wpool[:, 2 * T_FULL:2 * T_FULL + 4096],
                            vpool[:, 0:4096], vpool[:, 4096:8192]]):
                return nc

            p3b_st0, p3b_ss0, p3b_sv0, p3b_ld0 = st2.n, ss.n, sv.n, ld.n
            BL3 = []
            _st, _ss, _sv, _ld = p3b_st0, p3b_ss0, p3b_sv0, p3b_ld0
            for qi in range(16):
                for pj in range(2):
                    nkc = 4 * qi + 4
                    BL3.append((qi, pj, nkc, _st, _ss, _sv))
                    M[f"p3b_avend_{qi}_{pj}"] = _st + 2 * nkc
                    M[f"p3b_expend_{qi}_{pj}"] = _ss + nkc
                    M[f"p3b_nrm2_{qi}_{pj}"] = _sv + 8
                    _st += 2 * nkc + 2
                    _ss += nkc + 4
                    _sv += 8
                    _ld += 16
                    M[f"p3b_ow_{qi}_{pj}"] = _ld

            def t3b(t):
                for (qi, pj, nkc, st2_0, ss_0, sv_0) in BL3:
                    mA, mB = 2 * pj, 2 * pj + 1
                    first = (qi == 0 and pj == 0)
                    if first:
                        sv.wait(t, M["p3a_qke_15"])
                        ss.wait(t, M["p3a_ve_15"])
                    for kc in range(nkc):
                        par = kc % 2
                        if kc >= 2:
                            ss.wait(t, ss_0 + kc - 1)
                        elif not first:
                            ss.wait(t, M[f"p3b_expend_{qi}_{pj - 1}" if pj == 1
                                         else f"p3b_expend_{qi - 1}_1"])
                        t.matmul(ps_sA[par][:],
                                 lhsT=KTI[0:64, pj, 128 * kc:128 * (kc + 1)].opt(),
                                 rhs=QTI[0:64, pj, 512 * qi:512 * (qi + 1)].opt(),
                                 start=True, stop=True)
                        st2.inc(t.matmul(ps_sB[par][:],
                                         lhsT=KTI[64:128, pj, 128 * kc:128 * (kc + 1)].opt(),
                                         rhs=QTI[64:128, pj, 512 * qi:512 * (qi + 1)].opt(),
                                         start=True, stop=True, tile_position=(64, 0)))
                        kav = kc - 1
                        if kav >= 0:
                            parv = kav % 2
                            diag = kav - 4 * qi
                            if 0 <= diag < 4:
                                sv.wait(t, sv_0 + diag + 1)
                            else:
                                ss.wait(t, ss_0 + kav + 1)
                            if kav == 0:
                                k_no = M.get(f"p3b_nrm2_{qi}_{pj - 1}" if pj == 1
                                             else (f"p3b_nrm2_{qi - 1}_1" if qi > 0 else None))
                                if k_no is not None:
                                    sv.wait(t, k_no)
                            t.matmul(ps_oA[0:65, :],
                                     lhsT=V_I[:, kav, 65 * mA:65 * mA + 65].opt(),
                                     rhs=e_bf[2 * parv][:], start=(kav == 0),
                                     stop=(kav == nkc - 1))
                            st2.inc(t.matmul(ps_oB[0:65, :],
                                             lhsT=V_I[:, kav, 65 * mB:65 * mB + 65].opt(),
                                             rhs=e_bf[2 * parv + 1][:], start=(kav == 0),
                                             stop=(kav == nkc - 1)))
                    kav = nkc - 1
                    parv = kav % 2
                    sv.wait(t, sv_0 + 4)
                    t.matmul(ps_oA[0:65, :], lhsT=V_I[:, kav, 65 * mA:65 * mA + 65].opt(),
                             rhs=e_bf[2 * parv][:], start=(kav == 0), stop=True)
                    st2.inc(t.matmul(ps_oB[0:65, :],
                                     lhsT=V_I[:, kav, 65 * mB:65 * mB + 65].opt(),
                                     rhs=e_bf[2 * parv + 1][:], start=(kav == 0), stop=True))
                    for hh in range(2):
                        sv.wait(t, sv_0 + 4 + 2 * hh + 1)
                        if hh == 1:
                            ss.wait(t, ss_0 + nkc + 2)
                        st2.inc(t.matmul(ps_g1[0:64, :], lhsT=ones_f,
                                         rhs=rec_s, start=True, stop=True))
            T_(t3b)
            assert st2.n == _st

            def s3b(s):
                for (qi, pj, nkc, st2_0, ss_0, sv_0) in BL3:
                    for kc in range(nkc):
                        par = kc % 2
                        st2.wait(s, st2_0 + (1 if kc == 0 else 2 * kc))
                        s.activation(out=e_bf[2 * par][:], in_=ps_sA[par][:],
                                     func=AF.Exp, scale=0.125)
                        ss.inc(s.activation(out=e_bf[2 * par + 1][:], in_=ps_sB[par][:],
                                            func=AF.Exp, scale=0.125))
                    for hh in range(2):
                        ps_o = ps_oA if hh == 0 else ps_oB
                        st2.wait(s, st2_0 + 2 * nkc)
                        if hh == 1:
                            sv.wait(s, sv_0 + 4 + 2)
                        ss.inc(s.copy(out=den_s, in_=ps_o[64:65, :]))
                        st2.wait(s, st2_0 + 2 * nkc + hh + 1)
                        if hh == 1:
                            sv.wait(s, sv_0 + 4 + 3)
                        ss.inc(s.copy(out=recb, in_=ps_g1[0:64, :]))
            S_(s3b)
            assert ss.n == _ss

            def v3b(v):
                for (qi, pj, nkc, st2_0, ss_0, sv_0) in BL3:
                    for d2 in range(4):
                        kc = 4 * qi + d2
                        ss.wait(v, ss_0 + kc + 1)
                        v.tensor_tensor(out=e_bf[2 * (kc % 2)][:], in0=e_bf[2 * (kc % 2)][:],
                                        in1=masks[:, 512 * d2:512 * (d2 + 1)], op=AO.mult)
                        sv.inc(v.tensor_tensor(out=e_bf[2 * (kc % 2) + 1][:],
                                               in0=e_bf[2 * (kc % 2) + 1][:],
                                               in1=masks[:, 512 * d2:512 * (d2 + 1)],
                                               op=AO.mult))
                    for hh in range(2):
                        ps_o = ps_oA if hh == 0 else ps_oB
                        if qi == 0 and pj == 0 and hh == 0:
                            v.memset(ones_f, 1.0)
                        ss.wait(v, ss_0 + nkc + 2 * hh + 1)
                        sv.inc(v.reciprocal(out=rec_s, in_=den_s))
                        ss.wait(v, ss_0 + nkc + 2 * hh + 2)
                        if hh == 0 and not (qi == 0 and pj == 0):
                            ld.wait(v, M[f"p3b_ow_{qi}_{pj - 1}" if pj == 1
                                         else f"p3b_ow_{qi - 1}_1"])
                        sv.inc(v.tensor_tensor(out=ob_s[64 * hh:64 * hh + 64, :],
                                               in0=ps_o[0:64, :], in1=recb,
                                               op=AO.mult))
            V_(v3b)
            assert sv.n == _sv

            def y3b(y):
                for (qi, pj, nkc, st2_0, ss_0, sv_0) in BL3:
                    sv.wait(y, M[f"p3b_nrm2_{qi}_{pj}"])
                    ld.dma(y.dma_start(
                        out=ag2_in[256 * (qi // 4) + 128 * pj:256 * (qi // 4) + 128 * (pj + 1),
                                   512 * (qi % 4):512 * (qi % 4 + 1)],
                        in_=ob_s[:]))
            Y_(y3b)
            assert ld.n == _ld

            if dump("p3", []):
                return nc

            def _a2_split(g, ms):
                ld.wait(g, M[f"p3b_ow_{4 * ms + 3}_1"])
                cc.inc(g.collective_compute(
                    "AllGather", AO.bypass, replica_groups=RG,
                    ins=[ag2_in[256 * ms:256 * (ms + 1), :].opt()],
                    outs=[ag2_out[1024 * ms:1024 * (ms + 1), :].opt()]))

            def g_a2w(g):
                for ms in range(3):
                    _a2_split(g, ms)
                st2.wait(g, M["p3b_avend_15_1"])
                wld.dma(g.dma_start(out=woi_s, in_=woi_d.rearrange("(c p) n -> p c n", p=128)))
                _a2_split(g, 3)
            G_(g_a2w)
            M["woi_ld"] = wld.n
            M["ag2"] = cc.n

            p4_sv0, p4_ss0, p4_st0, p4_ld0, p4_f0 = sv.n, ss.n, st2.n, ld.n, fc2.n
            P4B = lambda c: hT_t[c % 2][:, 1024 * ((c // 2) % 2):
                                        1024 * ((c // 2) % 2) + 1024]

            def _p4_load(eng, regs, c):
                reg = regs[(c // 2) % 2] if isinstance(regs, tuple) else regs
                if c < 2:
                    cc.wait(eng, M["ag2"])
                if c >= 3 and c % 2 == 1:
                    fc2.wait(eng, p4_f0 + 16 * ((c - 1) // 2))
                if c >= 4:
                    st2.wait(eng, T4(c - 4, 1))
                eng.reg_load(reg, toff_s[0:1, c:c + 1])
                in_ap = AP(ag2_out.tensor, reg, [[2048, 128], [262144, 8], [1, 128]])
                ctr = ld if c % 2 == 0 else fc2
                ctr.dma(eng.dma_start(
                    out=P4B(c).rearrange("p (a b) -> p a b", a=8), in_=in_ap))
            V4 = lambda c, k: p4_sv0 + 5 * c + k
            S4 = lambda c, k: p4_ss0 + 4 * c + k
            T4 = lambda c, k: p4_st0 + 3 * c + k
            L4 = lambda c: (p4_ld0 if c % 2 == 0 else p4_f0) + 16 * (c // 2 + 1)

            def y_p4(y):
                for c in range(1, NCH, 2):
                    _p4_load(y, soff_reg, c)
            Y_(y_p4)
            assert fc2.n == p4_f0 + 16 * (NCH // 2)

            def s_p4(s):
                _p4_load(s, (toff_reg, toff_reg2), 0)
                _p4_load(s, (toff_reg, toff_reg2), 2)
                for c in range(NCH):
                    sv.wait(s, V4(c, 1))
                    ss.inc(s.activation(out=scr1k[:], in_=A(c), func=AF.Square,
                                        accum_out=scr16[:, 0:1]))
                    sv.wait(s, V4(c, 2))
                    pw.inc(s.activation(out=scr16[:, 2:3], in_=scr16[:, 1:2], func=AF.Ln))
                    pw.wait(s)
                    ss.inc(s.activation(out=scr16[:, 3:4], in_=scr16[:, 2:3],
                                        func=AF.Exp, scale=-0.5))
                    st2.wait(s, T4(c, 2))
                    ss.inc(s.copy(out=RT[:, :, 128 * c:128 * (c + 1)],
                                  in_=ps_tr[:].rearrange("p (c n) -> p c n", c=8)))
                    sv.wait(s, V4(c, 4))
                    ss.inc(s.activation(out=pr8[:], in_=lg8[:], func=AF.Exp))
                    if c % 2 == 0 and c + 4 < NCH:
                        _p4_load(s, (toff_reg, toff_reg2), c + 4)
            S_(s_p4)
            assert ss.n == p4_ss0 + 4 * NCH and ld.n == p4_ld0 + 16 * (NCH // 2)

            def t_p4(t):
                for c in range(NCH):
                    (ld if c % 2 == 0 else fc2).wait(t, L4(c))
                    if c == 0:
                        wld.wait(t, M["woi_ld"])
                    if c > 0:
                        sv.wait(t, V4(c - 1, 1))
                    for nh in range(2):
                        for j in range(8):
                            t.matmul(PS_G[nh][:, :],
                                     lhsT=P4B(c)[:, 128 * j:128 * (j + 1)],
                                     rhs=woi_s[:, j, 512 * nh:512 * (nh + 1)],
                                     start=(j == 0), stop=(j == 7))
                    st2.inc(t.last)
                    sv.wait(t, V4(c, 3))
                    if c > 0:
                        ss.wait(t, S4(c - 1, 3))
                    for j in range(8):
                        t.transpose(ps_tr[:, 128 * j:128 * (j + 1)],
                                    in_=r_bf[:, 128 * j:128 * (j + 1)], identity=ident[:])
                    st2.inc(t.last)
                    ss.wait(t, S4(c, 3))
                    if c > 0:
                        sv.wait(t, V4(c - 1, 4))
                    for dc in range(8):
                        t.matmul(ps_oA[:, 0:8], lhsT=RT[:, dc, 128 * c:128 * (c + 1)].opt(),
                                 rhs=rw_s[:, dc, :], start=(dc == 0), stop=(dc == 7))
                    st2.inc(t.last)
            T_(t_p4)
            assert st2.n == p4_st0 + 3 * NCH

            def v_p4(v):
                for c in range(NCH):
                    st2.wait(v, T4(c, 1))
                    for nh in range(2):
                        v.tensor_tensor(out=A(c)[:, 512 * nh:512 * (nh + 1)],
                                        in0=PS_G[nh][:, :],
                                        in1=A(c)[:, 512 * nh:512 * (nh + 1)], op=AO.add)
                    sv.inc(v.last)
                    ss.wait(v, S4(c, 1))
                    sv.inc(v.tensor_scalar(out=scr16[:, 1:2], in0=scr16[:, 0:1],
                                           scalar1=1.0 / D, scalar2=EPS,
                                           op0=AO.mult, op1=AO.add))
                    ss.wait(v, S4(c, 2))
                    sv.inc(v.tensor_scalar(out=r_bf, in0=A(c), scalar1=scr16[:, 3:4],
                                           scalar2=None, op0=AO.mult))
                    st2.wait(v, T4(c, 3))
                    pv.inc(v.tensor_copy(out=lg8[:], in_=ps_oA[:, 0:8]))
                    pv.wait(v)
                    pv.inc(v.reduce_max(out=tv16[:, 0:1], in_=lg8[:], axis=AX))
                    pv.wait(v)
                    sv.inc(v.tensor_scalar(out=lg8[:], in0=lg8[:], scalar1=tv16[:, 0:1],
                                           scalar2=None, op0=AO.subtract))
                    ss.wait(v, S4(c, 4))
                    pv.inc(v.reduce_sum(out=tv16[:, 1:2], in_=pr8[:], axis=AX))
                    pv.wait(v)
                    pv.inc(v.reciprocal(out=tv16[:, 2:3], in_=tv16[:, 1:2]))
                    pv.wait(v)
                    pv.inc(v.tensor_scalar(out=pr8[:], in0=pr8[:], scalar1=tv16[:, 2:3],
                                           scalar2=None, op0=AO.mult))
                    pv.wait(v)
                    pv.inc(v.reduce_max(out=tv16[:, 3:4], in_=pr8[:], axis=AX))
                    pv.wait(v)
                    pv.inc(v.tensor_scalar(out=msk8[:], in0=pr8[:], scalar1=tv16[:, 3:4],
                                           scalar2=None, op0=AO.is_ge))
                    pv.wait(v)
                    pv.inc(v.scalar_tensor_tensor(out=pm8[:], in0=msk8[:], scalar=-2.0,
                                                  in1=pr8[:], op0=AO.mult, op1=AO.add))
                    pv.wait(v)
                    pv.inc(v.reduce_max(out=tv16[:, 4:5], in_=pm8[:], axis=AX))
                    pv.wait(v)
                    pv.inc(v.tensor_tensor(out=tv16[:, 5:6], in0=tv16[:, 3:4],
                                           in1=tv16[:, 4:5], op=AO.add))
                    pv.wait(v)
                    pv.inc(v.reciprocal(out=tv16[:, 6:7], in_=tv16[:, 5:6]))
                    pv.wait(v)
                    v.tensor_tensor(out=tv16[:, 7:8], in0=tv16[:, 3:4], in1=tv16[:, 6:7],
                                    op=AO.mult)
                    pv.inc(v.tensor_tensor(out=tv16[:, 8:9], in0=tv16[:, 4:5],
                                           in1=tv16[:, 6:7], op=AO.mult))
                    pv.wait(v)
                    pv.inc(v.tensor_scalar(out=msk8b[:], in0=pm8[:], scalar1=tv16[:, 4:5],
                                           scalar2=None, op0=AO.is_ge))
                    pv.wait(v)
                    pv.inc(v.tensor_scalar(out=g_e[:, c, :].opt(), in0=msk8[:],
                                           scalar1=tv16[:, 7:8], scalar2=None, op0=AO.mult))
                    pv.wait(v)
                    sv.inc(v.scalar_tensor_tensor(out=g_e[:, c, :].opt(), in0=msk8b[:],
                                                  scalar=tv16[:, 8:9], in1=g_e[:, c, :].opt(),
                                                  op0=AO.mult, op1=AO.add))
            V_(v_p4)
            assert sv.n == p4_sv0 + 5 * NCH
            M["p4_rte_15"] = p4_ss0 + 4 * NCH
            M["p4_g_15"] = sv.n
            M["p4_wo_15"] = p4_st0 + 3 * (NCH - 1) + 1

            if dump("p4", [bufA[:]]):
                return nc
            if dump("d_g", [g_e[:, :, :].opt(), vpool[:, 0:2048], pr8[:], pm8[:], msk8[:], msk8b[:], tv16[:]]):
                return nc

            def _fc_slab(eng, ctr, idx):
                e, hb = idx // 2, idx % 2
                slot = idx % 2
                if idx == 0:
                    st2.wait(eng, M["p3b_avend_15_1"])
                elif idx == 1:
                    st2.wait(eng, M["p4_wo_15"])
                else:
                    st2.wait(eng, M.get(f"p5_cons_{idx - 2}", 0))
                ctr.dma(eng.dma_start(
                    out=FC1H[slot],
                    in_=fc1_d[D * e:D * (e + 1), 1024 * hb:1024 * (hb + 1)]
                    .rearrange("(c p) n -> p c n", p=128)))
                ctr.dma(eng.dma_start(
                    out=FC2H[slot],
                    in_=fc2_d[2048 * e + 1024 * hb:2048 * e + 1024 * (hb + 1), :]
                    .rearrange("(c p) n -> p c n", p=128)))

            def g_fc(g):
                for idx in range(16):
                    _fc_slab(g, fcl, idx)

            p5_cons_marks = {}
            fcl_marks = {}
            fcl_ctr = {}
            fcl_base = fcl.n
            for i in range(16):
                fcl_marks[i] = fcl_base + 32 * (i + 1)
                fcl_ctr[i] = fcl

            p5_st0, p5_ss0, p5_sv0 = st2.n, ss.n, sv.n
            BI = lambda e, hb, tb: 8 * (2 * e + hb) + tb
            for e in range(8):
                for hb in range(2):
                    idx = 2 * e + hb
                    for tb in range(8):
                        M[f"p5_ev_{e}_{hb}_{tb}"] = p5_sv0 + 9 * BI(e, hb, tb) + 9
                    p5_cons_marks[idx] = p5_st0 + 16 * 8 * (idx + 1)
                    M[f"p5_cons_{idx}"] = p5_cons_marks[idx]

            def t5_all(t):
                for e in range(8):
                    for hb in range(2):
                        idx = 2 * e + hb
                        slot = idx % 2
                        for tb in range(8):
                            i = BI(e, hb, tb)
                            st2_0 = p5_st0 + 16 * i
                            ss_0 = p5_ss0 + 8 * i
                            sv_0 = p5_sv0 + 9 * i
                            if idx == 0 and tb == 0:
                                fcl_ctr[0].wait(t, fcl_marks[0])
                                ss.wait(t, M["p4_rte_15"])
                                sv.wait(t, M["p4_g_15"])
                            elif tb == 0:
                                fcl_ctr[idx].wait(t, fcl_marks[idx])
                            for hc8 in range(8):
                                par = hc8 % 2
                                if hc8 >= 2:
                                    ss.wait(t, ss_0 + hc8 - 1)
                                for dc in range(8):
                                    t.matmul(ps_sA[par][:, 0:256],
                                             lhsT=FC1H[slot][:, dc, 128 * hc8:128 * (hc8 + 1)],
                                             rhs=RT[:, dc, 256 * tb:256 * (tb + 1)].opt(),
                                             start=(dc == 0), stop=(dc == 7))
                                st2.inc(t.last)
                                kg = hc8 - 1
                                if kg >= 0:
                                    parg = kg % 2
                                    sv.wait(t, sv_0 + kg + 1)
                                    if kg == 0:
                                        if tb > 0:
                                            k_ev = M.get(f"p5_ev_{e}_{hb}_{tb - 1}")
                                        elif hb == 1:
                                            k_ev = M.get(f"p5_ev_{e}_0_7")
                                        else:
                                            k_ev = M.get(f"p5_ev_{e - 1}_1_7" if e > 0 else None)
                                        if k_ev is not None:
                                            sv.wait(t, k_ev)
                                    accs = [[ps_oA, ps_oB], [ps_sB[0], ps_sB[1]]]
                                    for tc in range(2):
                                        for nh in range(2):
                                            t.matmul(accs[tc][nh][:, :],
                                                     lhsT=h1sq[:, 256 * parg + 128 * tc:
                                                               256 * parg + 128 * (tc + 1)],
                                                     rhs=FC2H[slot][:, kg, 512 * nh:512 * (nh + 1)],
                                                     start=(kg == 0), stop=False)
                                    st2.inc(t.last)
                            sv.wait(t, sv_0 + 8)
                            accs = [[ps_oA, ps_oB], [ps_sB[0], ps_sB[1]]]
                            for tc in range(2):
                                for nh in range(2):
                                    t.matmul(accs[tc][nh][:, :],
                                             lhsT=h1sq[:, 256 + 128 * tc:256 + 128 * (tc + 1)],
                                             rhs=FC2H[slot][:, 7, 512 * nh:512 * (nh + 1)],
                                             start=False, stop=True)
                            st2.inc(t.last)
            T_(t5_all)
            assert st2.n == p5_st0 + 16 * 128

            def s5_all(s):
                for i in range(128):
                    st2_0 = p5_st0 + 16 * i
                    ss_0 = p5_ss0 + 8 * i
                    sv_0 = p5_sv0 + 9 * i
                    for hc8 in range(8):
                        par = hc8 % 2
                        st2.wait(s, st2_0 + (1 if hc8 == 0 else 2 * hc8))
                        if hc8 >= 2:
                            sv.wait(s, sv_0 + hc8 - 1)
                        ss.inc(s.activation(out=scr1k[:, 256 * par:256 * (par + 1)],
                                            in_=ps_sA[par][:, 0:256], func=AF.Relu))
            S_(s5_all)
            assert ss.n == p5_ss0 + 8 * 128

            def v5_all(v):
                for e in range(8):
                    for hb in range(2):
                        for tb in range(8):
                            i = BI(e, hb, tb)
                            st2_0 = p5_st0 + 16 * i
                            ss_0 = p5_ss0 + 8 * i
                            for hc8 in range(8):
                                par = hc8 % 2
                                ss.wait(v, ss_0 + hc8 + 1)
                                if hc8 >= 2:
                                    st2.wait(v, st2_0 + 2 * (hc8 - 2) + 3)
                                sv.inc(v.tensor_tensor(
                                    out=h1sq[:, 256 * par:256 * (par + 1)],
                                    in0=scr1k[:, 256 * par:256 * (par + 1)],
                                    in1=scr1k[:, 256 * par:256 * (par + 1)], op=AO.mult))
                            st2.wait(v, st2_0 + 16)
                            accs = [[ps_oA, ps_oB], [ps_sB[0], ps_sB[1]]]
                            for tc in range(2):
                                ch = 2 * tb + tc
                                for nh in range(2):
                                    v.scalar_tensor_tensor(
                                        out=A(ch)[:, 512 * nh:512 * (nh + 1)],
                                        in0=accs[tc][nh][:, :], scalar=g_e[:, ch, e:e + 1],
                                        in1=A(ch)[:, 512 * nh:512 * (nh + 1)],
                                        op0=AO.mult, op1=AO.add)
                            sv.inc(v.last)
            V_(v5_all)
            assert sv.n == p5_sv0 + 9 * 128

            G_(g_fc)

            if dump("p5", [bufA[:]]):
                return nc

            p6_sv0, p6_ss0, p6_ld0, p6_sy0 = sv.n, ss.n, ld.n, sy.n
            V6 = lambda c, k: p6_sv0 + 3 * c + k
            S6 = lambda c, k: p6_ss0 + 2 * c + k
            L6 = lambda c: p6_ld0 + 16 * (c + 1)
            O6 = lambda c: p6_sy0 + 16 * (c + 1)

            def y_p6(y):
                for c in range(NCH):
                    if c == 0:
                        sv.wait(y, M["p5_ev_7_1_7"])
                        for cl in range(2):
                            ld.dma(y.dma_start(
                                out=x1_t[cl % 2],
                                in_=x1_d.rearrange("(c p) d -> p c d", p=128)[:, cl, :]))
                    if c + 2 < NCH:
                        sv.wait(y, V6(c, 3))
                        ld.dma(y.dma_start(
                            out=x1_t[c % 2],
                            in_=x1_d.rearrange("(c p) d -> p c d", p=128)[:, c + 2, :]))
                    else:
                        sv.wait(y, V6(c, 3))
                    sc = scr1k2 if c % 2 == 0 else scr1k
                    sy.dma(y.dma_start(
                        out=out_d.rearrange("(c p) d -> p c d", p=128)[:, c, :], in_=sc[:]))
            Y_(y_p6)
            assert sy.n == p6_sy0 + 16 * NCH

            def s_p6(s):
                for c in range(NCH):
                    if c == 0:
                        sv.wait(s, M["p5_ev_7_1_7"])
                    ss.inc(s.activation(out=a_nat[0], in_=A(c), func=AF.Square,
                                        accum_out=sumsq[:, c:c + 1]))
                    sv.wait(s, V6(c, 2))
                    pw.inc(s.activation(out=scr16[:, 1:2], in_=var_v[:, c:c + 1], func=AF.Ln))
                    pw.wait(s)
                    ss.inc(s.activation(out=r1_v[:, c:c + 1], in_=scr16[:, 1:2],
                                        func=AF.Exp, scale=-0.5))
            S_(s_p6)
            assert ss.n == p6_ss0 + 2 * NCH

            def v_p6(v):
                for c in range(NCH):
                    sv.inc(v.reduce_sum(out=sumv[:, c:c + 1], in_=A(c), axis=AX))
                    ss.wait(v, S6(c, 1))
                    sl = slice(c, c + 1)
                    v.tensor_scalar(out=mean_v[:, sl], in0=sumv[:, sl], scalar1=1.0 / D,
                                    scalar2=None, op0=AO.mult)
                    pv.inc(v.tensor_scalar(out=var_v[:, sl], in0=sumsq[:, sl],
                                           scalar1=1.0 / D, scalar2=None, op0=AO.mult))
                    pv.wait(v)
                    pv.inc(v.tensor_tensor(out=scr16[:, 0:1], in0=mean_v[:, sl],
                                           in1=mean_v[:, sl], op=AO.mult))
                    pv.wait(v)
                    pv.inc(v.tensor_tensor(out=var_v[:, sl], in0=var_v[:, sl],
                                           in1=scr16[:, 0:1], op=AO.subtract))
                    pv.wait(v)
                    sv.inc(v.tensor_scalar(out=var_v[:, sl], in0=var_v[:, sl], scalar1=EPS,
                                           scalar2=None, op0=AO.add))
                    ss.wait(v, S6(c, 2))
                    ld.wait(v, L6(c))
                    if c >= 2:
                        sy.wait(v, O6(c - 2))
                    sc = scr1k2 if c % 2 == 0 else scr1k
                    v.tensor_scalar(out=sc[:], in0=A(c), scalar1=mean_v[:, c:c + 1],
                                    scalar2=r1_v[:, c:c + 1], op0=AO.subtract, op1=AO.mult)
                    sv.inc(v.tensor_tensor(out=sc[:], in0=sc[:], in1=x1_t[c % 2],
                                           op=AO.add))
            V_(v_p6)
            assert sv.n == p6_sv0 + 3 * NCH

            def y_fin(y):
                sy.wait(y)
            Y_(y_fin)

    return nc


def make_consts():
    masks = np.zeros((128, 4 * 512), dtype=np.float32)
    for d in range(4):
        kk = np.arange(128)[:, None]
        qq = np.arange(512)[None, :]
        masks[:, 512 * d:512 * (d + 1)] = (128 * d + kk <= qq).astype(np.float32)
    kvec = np.tile(np.arange(1, 33, dtype=np.float32), 16)[None, :].repeat(128, 0)
    identb = np.eye(128, dtype=ml_dtypes.bfloat16)
    identf = np.eye(128, dtype=np.float32)
    return masks, np.ascontiguousarray(kvec), identb, identf


_PROG_CACHE = {}


def get_program(debug_phase=None):
    key = debug_phase
    if key not in _PROG_CACHE:
        _PROG_CACHE[key] = build_program(debug_phase)
    return _PROG_CACHE[key]


def make_in_maps(inputs):
    masks, kvec, identb, identf = make_consts()
    f32 = np.float32
    x = np.asarray(inputs["L0_attn_query"], f32)
    kp = np.ascontiguousarray(np.asarray(inputs["L0_attn_key_param"], f32)[0])
    base = dict(
        kp=kp,
        wqo=np.asarray(inputs["L0_attn_Wq"], f32),
        wko=np.asarray(inputs["L0_attn_Wk"], f32),
        wvo=np.asarray(inputs["L0_attn_Wv"], f32),
        woo=np.asarray(inputs["L0_attn_Wo"], f32),
        bq=np.ascontiguousarray(np.asarray(inputs["L0_attn_bq"], f32).reshape(8, 128)),
        bk=np.ascontiguousarray(np.asarray(inputs["L0_attn_bk"], f32).reshape(8, 128)),
        bv=np.ascontiguousarray(np.asarray(inputs["L0_attn_bv"], f32).reshape(1, D)),
        bo=np.ascontiguousarray(np.asarray(inputs["L0_attn_bo"], f32).reshape(1, D)),
        woi=np.asarray(inputs["L0_ffn_wo"], f32),
        rw=np.asarray(inputs["L0_ffn_router_w"], f32),
        fc1=np.ascontiguousarray(np.asarray(inputs["L0_ffn_exp_fc1"], f32).reshape(8 * D, 2048)),
        fc2=np.ascontiguousarray(np.asarray(inputs["L0_ffn_exp_fc2"], f32).reshape(8 * 2048, D)),
        masks=masks, kvec=kvec, ident_bf=identb, ident_f32=identf,
    )
    wq = np.asarray(inputs["L0_ffn_wq"], f32)
    wk = np.asarray(inputs["L0_ffn_wk"], f32)
    wv = np.asarray(inputs["L0_ffn_wv"], f32)
    in_maps = []
    for c in range(8):
        g, q = c // 4, c % 4
        m = dict(base)
        m["x0"] = np.ascontiguousarray(x[g, T_CORE * q:T_CORE * (q + 1)])
        m["wqi"] = np.ascontiguousarray(wq[:, 256 * q:256 * (q + 1)])
        m["wki"] = np.ascontiguousarray(wk[:, 256 * q:256 * (q + 1)])
        m["wvi"] = np.ascontiguousarray(wv[:, 256 * q:256 * (q + 1)])
        m["toffs"] = np.asarray([[2097152 * q + 128 * cc for cc in range(16)]],
                                dtype=np.int32)
        m["soffs"] = np.asarray([[2097152 * (cc % 4) + 64 * (4 * q + cc // 4)
                                  for cc in range(16)]], dtype=np.int32)
        in_maps.append(m)
    return in_maps


def kernel(**inputs):
    nc = get_program()
    in_maps = make_in_maps(inputs)
    res = run_bass_kernel_spmd(nc, in_maps, core_ids=list(range(8)))
    out = np.empty((2, T_FULL, D), dtype=np.float32)
    for c in range(8):
        g, q = c // 4, c % 4
        out[g, T_CORE * q:T_CORE * (q + 1)] = res.results[c]["out"]
    return out
